# version 16
# baseline (speedup 1.0000x reference)
import contextlib
import numpy as np
import ml_dtypes
import concourse.bass as bass
import concourse.mybir as mybir
from concourse.bass_utils import run_bass_kernel_spmd
from concourse.alu_op_type import AluOpType
from concourse.aot_env import allow_ici_all_to_all

F32 = mybir.dt.float32
BF16 = mybir.dt.bfloat16
I32 = mybir.dt.int32
ACT = mybir.ActivationFunctionType
AXX = mybir.AxisListType.X
NBF = ml_dtypes.bfloat16
MULT = AluOpType.mult
ADD = AluOpType.add
SUB = AluOpType.subtract
AMAX = AluOpType.max
AMIN = AluOpType.min

B, T, M, H, D = 2, 8192, 1024, 8, 128
NC = 8
SL = T // NC
NT = 16
IDIM = 2816
ICH = 22
FCH = 8
QG = 1024
NG = T // QG
RSC = float(D) ** -0.5


class Sem:
    def __init__(self, es, nc, name):
        self.h = es.enter_context(nc.semaphore(name))
        self.n = 0


class Sched:
    ENGS = ("t", "s", "v", "q", "g")

    def __init__(self):
        self.ops = {e: [] for e in self.ENGS}

    def op(self, ek, waits, emitfn, inc=None, inck=1):
        wl = [(s.h, v) for (s, v) in waits if v > 0]
        if inc is not None:
            inc.n += inck

        def thunk(eng, wl=wl, emitfn=emitfn, inc=inc, inck=inck):
            for hh, vv in wl:
                eng.wait_ge(hh, vv)
            instr = emitfn(eng)
            if inc is not None:
                instr.then_inc(inc.h, inck)
        self.ops[ek].append(thunk)

    def dma(self, ek, waits, out, in_, inc):
        self.op(ek, waits, lambda e, o=out, i=in_: e.dma_start(out=o, in_=i),
                inc, 16)


def _tables():
    w32 = np.arange(32)
    ang = 10000.0 ** (-w32 / 32.0)
    rad = np.arange(T)[:, None] * ang[None, :]
    cosT = np.concatenate([np.cos(rad).T, np.cos(rad).T], 0).astype(NBF)
    sinT = np.concatenate([-np.sin(rad).T, np.sin(rad).T], 0).astype(NBF)
    kk = np.arange(128)[:, None]
    jj = np.arange(2 * QG)[None, :]
    band = (jj >= kk + QG).astype(NBF)
    return cosT, sinT, band


def build(debug=()):
    nc = bass.Bass("TRN2", target_bir_lowering=False, debug=False)

    xb = nc.dram_tensor("xb", [T, M], F32, kind="ExternalInput").ap()
    xres = nc.dram_tensor("xres", [B, SL, M], F32, kind="ExternalInput").ap()
    wq3 = nc.dram_tensor("wq3", [M, 2, D], F32, kind="ExternalInput").ap()
    wk3 = nc.dram_tensor("wk3", [M, 2, D], F32, kind="ExternalInput").ap()
    wv3 = nc.dram_tensor("wv3", [M, 2, D], F32, kind="ExternalInput").ap()
    wo4 = nc.dram_tensor("wo4", [H, D, M], F32, kind="ExternalInput").ap()
    wgd = nc.dram_tensor("wgd", [IDIM, M], F32, kind="ExternalInput").ap()
    wud = nc.dram_tensor("wud", [IDIM, M], F32, kind="ExternalInput").ap()
    wdd = nc.dram_tensor("wdd", [M, IDIM], F32, kind="ExternalInput").ap()
    clips = nc.dram_tensor("clips", [1, 2], F32, kind="ExternalInput").ap()
    y_out = nc.dram_tensor("y", [B, SL, M], F32, kind="ExternalOutput").ap()

    a2a_in = [nc.dram_tensor(f"a2a_in{hl}", [NC, 128, QG], BF16, kind="Internal").ap()
              for hl in range(2)]
    a2a_out = [nc.dram_tensor(f"a2a_out{hl}", [NC, 128, QG], BF16, kind="Internal").ap()
               for hl in range(2)]
    cc_in = nc.dram_tensor("cc_in", [1, 1], F32, kind="Internal").ap()
    cc_out = nc.dram_tensor("cc_out", [1, 1], F32, kind="Internal").ap()
    cc2_in = nc.dram_tensor("cc2_in", [1, 1], F32, kind="Internal").ap()
    cc2_out = nc.dram_tensor("cc2_out", [1, 1], F32, kind="Internal").ap()
    z_dram = nc.dram_tensor("z_dram", [NT, 128, M], F32, kind="Internal").ap()
    wdq_dram = nc.dram_tensor("wdq_dram", [FCH, 128, IDIM], BF16, kind="Internal").ap()

    cosT_np, sinT_np, band_np = _tables()
    c_cos = nc.inline_tensor(cosT_np, "c_cos").ap()
    c_sin = nc.inline_tensor(sinT_np, "c_sin").ap()
    c_band = nc.inline_tensor(band_np, "c_band").ap()
    c_identB = nc.inline_tensor(np.eye(128, dtype=NBF), "c_identB").ap()
    c_identF = nc.inline_tensor(np.eye(128, dtype=np.float32), "c_identF").ap()
    c_onesB = nc.inline_tensor(np.ones((128, 1), dtype=NBF), "c_onesB").ap()
    c_onesF = nc.inline_tensor(np.ones((1, 128), dtype=np.float32), "c_onesF").ap()
    c_eps = nc.inline_tensor(np.full((128, 1), 1e-5, dtype=np.float32), "c_eps").ap()

    dbg = {}
    for name, shape in (("z", [NT, 128, M]), ("zpre", [NT, 128, M]),
                        ("sgc", [128, 4 * ICH]),
                        ("swd", [128, 32]), ("mps", [128, 512]),
                        ("row", [1, 1024]), ("wmaxd", [128, 8]),
                        ("wdt", [128, ICH * 1024]),
                        ("hh", [128, ICH * 2048]),
                        ("gmax", [1, 8]), ("q0", [128, T]), ("o0", [128, T]),
                        ("q1", [128, T]), ("k1", [128, T]), ("o1", [128, T]),
                        ("oall", [128, 16 * QG]), ("xq", [128, FCH * 2048])):
        if name in debug:
            dbg[name] = nc.dram_tensor("dbg_" + name, shape, F32,
                                       kind="ExternalOutput").ap()

    K = Sched()
    es = contextlib.ExitStack()
    with es:
        E = es.enter_context
        identB = E(nc.sbuf_tensor("identB", [128, 128], BF16))
        identF = E(nc.sbuf_tensor("identF", [128, 128], F32))
        onesB = E(nc.sbuf_tensor("onesB", [128, 1], BF16))
        onesF = E(nc.sbuf_tensor("onesF", [1, 128], F32))

        sw = E(nc.sbuf_tensor("sw", [128, 32], F32))
        rw2 = E(nc.sbuf_tensor("rw2", [1, 64], F32))
        clip_sb = E(nc.sbuf_tensor("clip_sb", [1, 2], F32))
        eps_col = E(nc.sbuf_tensor("eps_col", [128, 1], F32))
        wmaxd = E(nc.sbuf_tensor("wmaxd", [128, 8], F32))

        MEAN, SSQ, VAR, T1, INV = 0, 1, 2, 3, 4
        ZMAX, RXC, SXD, WMX, SGC, SUC = 5, 6, 7, 8, 9, 10
        HMAX, RHC, SHD, SDC = 11, 12, 13, 14
        MEAN0, STD0 = 16, 18

        S = {}
        for name in ("sC sX sW sT sE sQ sQE sVE sSw sR sM sEx sMk sPv sRc "
                     "sNm sA sCc sOa sWo sZv sSt sZs sXq sTq sDe sG sSi sH "
                     "sWq sWt sWe sD sY sHr sGl sXC sHs sWS sSg "
                     "sWX0 sWX1 sWX2 sXR0 sXR1 sZS0 sZS1 sZR0 sZR1 sZD0 sZD1 "
                     "sLG sLU sDF0 sDF1 sWO sGB sA0 sA1 sTB sYP0 sYP1 sDQ sDW0 sDW1 sWR").split():
            S[name] = Sem(es, nc, name)
        sC, sX, sW, sT, sE = S["sC"], S["sX"], S["sW"], S["sT"], S["sE"]
        sQ, sQE, sVE, sSw, sR = S["sQ"], S["sQE"], S["sVE"], S["sSw"], S["sR"]
        sM, sEx, sMk, sPv, sRc = S["sM"], S["sEx"], S["sMk"], S["sPv"], S["sRc"]
        sNm, sA, sCc, sOa, sWo = S["sNm"], S["sA"], S["sCc"], S["sOa"], S["sWo"]
        sZv, sSt, sZs, sXq, sTq = S["sZv"], S["sSt"], S["sZs"], S["sXq"], S["sTq"]
        sDe, sG, sSi, sH, sWq = S["sDe"], S["sG"], S["sSi"], S["sH"], S["sWq"]
        sWt, sWe, sD, sY, sHr, sGl = (S["sWt"], S["sWe"], S["sD"], S["sY"],
                                      S["sHr"], S["sGl"])
        sXC = S["sXC"]
        sHs = S["sHs"]
        sWS = S["sWS"]
        sSg = S["sSg"]
        sWX = [S["sWX0"], S["sWX1"], S["sWX2"]]
        sXR = [S["sXR0"], S["sXR1"]]
        sZS = [S["sZS0"], S["sZS1"]]
        sZR = [S["sZR0"], S["sZR1"]]
        sZD = [S["sZD0"], S["sZD1"]]
        sLG, sLU = S["sLG"], S["sLU"]
        sDF = [S["sDF0"], S["sDF1"]]
        sWO, sGB, sTB = S["sWO"], S["sGB"], S["sTB"]
        sAp = [S["sA0"], S["sA1"]]
        sYP = [S["sYP0"], S["sYP1"]]
        sDQ = S["sDQ"]
        sDW = [S["sDW0"], S["sDW1"]]
        sWR = S["sWR"]

        block = E(nc.Block())

        for dst, src in ((identB[:], c_identB), (identF[:], c_identF),
                         (onesB[:], c_onesB), (onesF[:], c_onesF),
                         (clip_sb[:], clips),
                         (eps_col[:], c_eps)):
            K.dma("q", [], dst, src, sC)

        att = es.enter_context(contextlib.ExitStack())
        EA = att.enter_context
        band = EA(nc.sbuf_tensor("band", [128, 2 * QG], BF16))
        rw = EA(nc.sbuf_tensor("rw", [1, 2 * QG], F32))
        cosT = EA(nc.sbuf_tensor("cosT", [64, T // 4], BF16))
        sinT = EA(nc.sbuf_tensor("sinT", [64, T // 4], BF16))
        wq_sb = EA(nc.sbuf_tensor("wq_sb", [128, 2 * FCH * D], BF16))
        wk_sb = EA(nc.sbuf_tensor("wk_sb", [128, 2 * FCH * D], BF16))
        wv_mv = EA(nc.sbuf_tensor("wv_mv", [128, FCH * 2 * D], BF16))
        QF = [EA(nc.sbuf_tensor(f"QF{h}", [128, T], BF16)) for h in range(2)]
        KF = [EA(nc.sbuf_tensor(f"KF{h}", [128, T], BF16)) for h in range(2)]
        VT = [EA(nc.sbuf_tensor(f"VT{h}", [128, T], BF16)) for h in range(2)]
        swp = EA(nc.sbuf_tensor("swp", [64, T // 4], BF16))
        tmp2 = EA(nc.sbuf_tensor("tmp2", [64, T // 4], BF16))
        pt = [EA(nc.sbuf_tensor(f"pt{i}", [128, QG], BF16)) for i in range(3)]
        obuf = [EA(nc.sbuf_tensor(f"obuf{i}", [128, QG], BF16)) for i in range(2)]
        rbc = EA(nc.sbuf_tensor("rbc", [128, QG], BF16))

        K.dma("q", [], band[:], c_band, sC)
        n_const = sC.n

        for h in range(2):
            for fc in range(FCH):
                K.dma("g", [], wq_sb[:, (h * FCH + fc) * D:(h * FCH + fc + 1) * D],
                      wq3[fc * 128:(fc + 1) * 128, h, :], sX)
                K.dma("g", [], wk_sb[:, (h * FCH + fc) * D:(h * FCH + fc + 1) * D],
                      wk3[fc * 128:(fc + 1) * 128, h, :], sX)
        for fc in range(FCH):
            K.dma("g", [], wv_mv[:, fc * 256:(fc + 1) * 256],
                  wv3[fc * 128:(fc + 1) * 128, :, :], sX)
        n_wqkv = sX.n

        xp = att.enter_context(contextlib.ExitStack())
        EX = xp.enter_context
        xtm = [EX(nc.sbuf_tensor(f"xtm{i}", [128, 4096], BF16)) for i in range(2)]
        xf32 = [EX(nc.sbuf_tensor(f"xf32{i}", [128, 1024], F32)) for i in range(2)]
        xfm = [EX(nc.sbuf_tensor(f"xfm{i}", [128, FCH * 512], BF16)) for i in range(2)]
        xt_ps = [EX(nc.psum_tensor(f"xt_ps{i}", [128, 1024], BF16)) for i in range(2)]
        q_ps = [EX(nc.psum_tensor(f"q_ps{h}", [128, 512], F32)) for h in range(2)]
        k_ps = [EX(nc.psum_tensor(f"k_ps{h}", [128, 512], F32)) for h in range(2)]
        v_ps = [EX(nc.psum_tensor(f"v_ps{i}", [128, 512], F32)) for i in range(2)]

        TH = T // 4
        rope_done = {}

        def emit_rope_pair(qt, pr):
            cs = slice(qt * TH, (qt + 1) * TH)
            evac_need = 16 * (qt + 1)
            TTl, h = ((0, 0), (1, 0), (0, 1), (1, 1))[pr]
            TT = (QF, KF)[TTl]
            if pr == 0:
                wtb = [] if sR.n == 0 else [(sR, sR.n - 1)]
                K.dma("g", wtb, cosT[:], c_cos[:, cs], sTB)
                K.dma("g", [], sinT[:], c_sin[:, cs], sTB)
            ntab = sTB.n
            w0 = [(sQE, evac_need)] if sR.n == 0 else [(sR, sR.n - 2),
                                                       (sQE, evac_need)]
            K.dma("g", w0, swp[0:32, :], TT[h][32:64, cs], sSw)
            K.dma("g", [], swp[32:64, :], TT[h][0:32, cs], sSw)
            K.op("v", [(sSw, sSw.n), (sTB, ntab)],
                 lambda v: v.tensor_mul(tmp2[:], swp[:], sinT[:]), sR)
            K.op("v", [],
                 lambda v, TT=TT, h=h, cs=cs: v.tensor_mul(
                     TT[h][0:64, cs], TT[h][0:64, cs], cosT[:]), sR)
            K.op("v", [],
                 lambda v, TT=TT, h=h, cs=cs: v.tensor_add(
                     TT[h][0:64, cs], TT[h][0:64, cs], tmp2[:]), sR)
            rope_done[(TTl, h)] = max(rope_done.get((TTl, h), 0), sR.n)

        for tg in range(16):
            for t4 in range(4):
                ii = tg * 4 + t4
                r0 = tg * 512 + t4 * 128
                K.dma("q", [(sXC, sXC.n - 1)],
                      xf32[ii % 2][:], xb[r0:r0 + 128, :], sWX[ii % 2])
                K.op("s", [(sWX[ii % 2], 16 * (ii // 2 + 1)), (sT, sT.n - 8)],
                     lambda ss, tg=tg, t4=t4, ii=ii: ss.copy(
                         xtm[tg % 2][:, t4 * 1024:(t4 + 1) * 1024], xf32[ii % 2][:]),
                     sXC)
            for fc in range(FCH):
                def tr(t, tg=tg, fc=fc):
                    for t4 in range(4):
                        mm = t.transpose(
                            xt_ps[fc % 2][:, t4 * 128:(t4 + 1) * 128],
                            xtm[tg % 2][:, t4 * 1024 + fc * 128:t4 * 1024 + (fc + 1) * 128],
                            identB[:])
                    return mm
                K.op("t", [(sXC, sXC.n), (sE, sE.n - 1), (sC, n_const)], tr, sT)
                K.op("v", [(sT, sT.n), (sQ, sQ.n - 6)],
                     lambda v, tg=tg, fc=fc: v.tensor_copy(
                         xfm[tg % 2][:, fc * 512:(fc + 1) * 512], xt_ps[fc % 2][:, 0:512]),
                     sE)
            for h in range(2):
                def qk(t, tg=tg, h=h):
                    for fc in range(FCH):
                        t.matmul(q_ps[h][:],
                                 wq_sb[:, (h * FCH + fc) * D:(h * FCH + fc + 1) * D],
                                 xfm[tg % 2][:, fc * 512:(fc + 1) * 512],
                                 start=(fc == 0), stop=(fc == FCH - 1))
                    for fc in range(FCH):
                        mm = t.matmul(k_ps[h][:],
                                      wk_sb[:, (h * FCH + fc) * D:(h * FCH + fc + 1) * D],
                                      xfm[tg % 2][:, fc * 512:(fc + 1) * 512],
                                      start=(fc == 0), stop=(fc == FCH - 1))
                    return mm
                K.op("t", [(sE, sE.n), (sQE, sQE.n - 2), (sX, n_wqkv)], qk, sQ)
                K.op("s", [(sQ, sQ.n)],
                     lambda s, tg=tg, h=h: s.mul(
                         QF[h][:, tg * 512:(tg + 1) * 512], q_ps[h][:], RSC), sQE)
                K.op("s", [],
                     lambda s, tg=tg, h=h: s.mul(
                         KF[h][:, tg * 512:(tg + 1) * 512], k_ps[h][:], RSC), sQE)
            for tch in range(4):
                def vv(t, tg=tg, tch=tch):
                    for fc in range(FCH):
                        mm = t.matmul(
                            v_ps[tch % 2][:, 0:256],
                            xfm[tg % 2][:, fc * 512 + tch * 128:fc * 512 + (tch + 1) * 128],
                            wv_mv[:, fc * 256:(fc + 1) * 256],
                            start=(fc == 0), stop=(fc == FCH - 1))
                    return mm
                K.op("t", [(sVE, sVE.n - 1)], vv, sQ)

                def vev(v, tg=tg, tch=tch):
                    for h in range(2):
                        mm = v.tensor_copy(
                            VT[h][:, (tg * 4 + tch) * 128:(tg * 4 + tch + 1) * 128],
                            v_ps[tch % 2][:, h * 128:(h + 1) * 128])
                    return mm
                K.op("v", [(sQ, sQ.n)], vev, sVE)
            for qt in range(4):
                pr = tg - (4 * qt + 3)
                if 0 <= pr <= 3:
                    emit_rope_pair(qt, pr)

        for pr in range(1, 4):
            emit_rope_pair(3, pr)

        n_qkdone = sQE.n
        n_ve = sVE.n
        xp.close()

        n_rope = sR.n
        rope_unit = {0: rope_done[(1, 0)], 1: n_rope}

        if "q0" in dbg:
            K.dma("g", [(sR, n_rope)], dbg["q0"], QF[0][:], sY)
        if "q1" in dbg:
            K.dma("g", [(sR, n_rope)], dbg["q1"], QF[1][:], sY)
        if "k1" in dbg:
            K.dma("g", [(sR, n_rope)], dbg["k1"], KF[1][:], sY)

        fl = att.enter_context(contextlib.ExitStack())
        EF = fl.enter_context
        st_ps = [EF(nc.psum_tensor(f"st_ps{i}", [128, QG], F32)) for i in range(2)]
        o_ps = EF(nc.psum_tensor("o_ps", [128, QG], F32))
        rsA = EF(nc.psum_tensor("rsA", [128, 512], F32))
        rsB = EF(nc.psum_tensor("rsB", [128, 512], F32))
        wdl3 = [EF(nc.sbuf_tensor(f"wdl3{i}", [128, 1024], F32)) for i in range(3)]
        wscr = EF(nc.sbuf_tensor("wscr", [128, 1024], F32))
        wscri = EF(nc.sbuf_tensor("wscri", [128, 1024], I32))
        wdqC = [EF(nc.sbuf_tensor(f"wdqC{i}", [128, 1024], BF16)) for i in range(2)]

        WD_CH = ((0, 1024), (1024, 2048), (2048, IDIM))
        base_ws, base_df, base_wr = sWS.n, sDF[0].n, sWR.n
        wd_steps = []
        for hc_ in range(FCH):
            wd_steps += [("L", hc_, ci) for ci in range(3)]
            wd_steps += [("R", hc_, ci) for ci in range(3)]
            wd_steps += [("W", hc_, 0)]
            wd_steps += [("S", hc_, ci) for ci in range(3)]

        def wd_step(st):
            kind, hc, ci = st
            c0, c1 = WD_CH[ci]
            wdt = c1 - c0
            if kind == "L":
                w = []
                if hc == 0 and ci == 0:
                    w = [(sQE, n_qkdone), (sVE, n_ve)]
                elif hc > 0:
                    w = [(sWS, base_ws + 3 * (hc - 1) + ci + 1)]
                K.dma("q", w, wdl3[ci][:, 0:wdt],
                      wdd[hc * 128:(hc + 1) * 128, c0:c1], sDF[0])
            elif kind == "R":
                def wred(v, hc=hc, ci=ci, wdt=wdt):
                    v.reduce_max(sw[:, 25:26], wdl3[ci][:, 0:wdt],
                                 axis=AXX, apply_absolute_value=True)
                    v.drain()
                    if ci == 0:
                        return v.tensor_copy(wmaxd[:, hc:hc + 1], sw[:, 25:26])
                    return v.tensor_max(wmaxd[:, hc:hc + 1],
                                        wmaxd[:, hc:hc + 1], sw[:, 25:26])
                K.op("v", [(sDF[0], base_df + 16 * (3 * hc + ci + 1))], wred, sWR)
            elif kind == "W":
                def wrecip(v, hc=hc):
                    v.drain()
                    v.reciprocal(sw[:, 24:25], wmaxd[:, hc:hc + 1])
                    v.drain()
                    return v.tensor_scalar_mul(sw[:, 24:25], sw[:, 24:25], 127.0)
                w = [(sWS, base_ws + 3 * hc)] if hc > 0 else []
                K.op("v", w, wrecip, sWR)
            else:
                def wds_(ss, hc=hc, ci=ci, wdt=wdt):
                    ss.activation(wscr[:, 0:wdt], wdl3[ci][:, 0:wdt],
                                  ACT.Copy, scale=sw[:, 24:25])
                    ss.copy(wscri[:, 0:wdt], wscr[:, 0:wdt])
                    return ss.copy(wdqC[ci % 2][:, 0:wdt], wscri[:, 0:wdt])
                K.op("s", [(sWR, base_wr + 4 * (hc + 1)),
                           (sDQ, sDQ.n - 16)], wds_, sWS)
                K.dma("q", [(sWS, sWS.n)], wdq_dram[hc, :, c0:c1],
                      wdqC[ci % 2][:, 0:wdt], sDQ)

        mask_cnt_after = {}

        def emit_st(hl, g, kc, first_flash):
            nn = sM.n

            def st(t, hl=hl, g=g, kc=kc, nn=nn):
                ks = kc * 128
                t.matmul(st_ps[nn % 2][:, 0:512], KF[hl][:, ks:ks + 128],
                         QF[hl][:, g * QG:g * QG + 512], start=True, stop=True)
                return t.matmul(st_ps[nn % 2][:, 512:QG], KF[hl][:, ks:ks + 128],
                                QF[hl][:, g * QG + 512:(g + 1) * QG],
                                start=True, stop=True)
            w = [(sEx, sEx.n - 1)]
            if first_flash is not None:
                w += [(sR, rope_unit[hl])]
                if first_flash:
                    w.append((sVE, n_ve))
                    w.append((sQE, n_qkdone))
            K.op("t", w, st, sM)

        def emit_exp():
            nn = sEx.n
            K.op("s", [(sM, sM.n), (sPv, sPv.n)],
                 lambda s, nn=nn: s.activation(pt[nn % 3][:], st_ps[nn % 2][:],
                                               ACT.Exp), sEx)

        def emit_mask(g, kc, nglob):
            off = QG - (kc * 128 - g * QG)
            w = [(sEx, sEx.n)] + ([(sC, n_const)] if nglob == 0 else [])
            K.op("v", w,
                 lambda v, nn=nglob, off=off: v.tensor_mul(
                     pt[nn % 3][:], pt[nn % 3][:], band[:, off:off + QG]), sMk)
            mask_cnt_after[nglob] = sMk.n

        def emit_rspv(hl, g, kc, diag, first, last, nglob, gidx):
            if diag:
                w = [(sMk, mask_cnt_after[nglob])]
            else:
                w = [(sEx, nglob + 1)]
            if first and gidx > 0:
                w += [(sNm, sNm.n), (sRc, sRc.n)]

            def rp(t, hl=hl, kc=kc, nn=nglob, first=first, last=last):
                pp = pt[nn % 3]
                ks = kc * 128
                t.matmul(rsA[0:1, 0:512], onesB[:], pp[:, 0:512],
                         start=first, stop=last)
                t.matmul(rsB[0:1, 0:512], onesB[:], pp[:, 512:QG],
                         start=first, stop=last)
                t.matmul(o_ps[:, 0:512], VT[hl][:, ks:ks + 128],
                         pp[:, 0:512], start=first, stop=last)
                return t.matmul(o_ps[:, 512:QG], VT[hl][:, ks:ks + 128],
                                pp[:, 512:QG], start=first, stop=last)
            K.op("t", w, rp, sPv)

        def emit_tail(hl, g, gidx):
            def v1(v):
                v.tensor_copy(rw[0:1, 0:512], rsA[0:1, 0:512])
                return v.tensor_copy(rw[0:1, 512:QG], rsB[0:1, 0:512])
            K.op("v", [(sPv, sPv.n)], v1, sRc)

            def bc(t):
                t.matmul(rsA[:, 0:512], onesF[:], rw[0:1, 0:512],
                         start=True, stop=True)
                return t.matmul(rsB[:, 0:512], onesF[:], rw[0:1, 512:QG],
                                start=True, stop=True)
            K.op("t", [(sRc, sRc.n)], bc, sRc)

            def v2(v):
                with nc.allow_low_precision(reason="bf16 recip == old f32 recip + bf16 copy"):
                    v.reciprocal(rbc[:, 0:512], rsA[:])
                    return v.reciprocal(rbc[:, 512:QG], rsB[:])
            K.op("v", [(sRc, sRc.n)], v2, sRc)
            K.op("v", [(sAp[gidx % 2], 16 * max(0, (gidx - 2) // 2 + 1))],
                 lambda v, gidx=gidx: v.tensor_mul(obuf[gidx % 2][:], o_ps[:],
                                                   rbc[:]), sNm)
            K.dma("g", [(sNm, sNm.n)], a2a_in[hl][g], obuf[gidx % 2][:],
                  sAp[gidx % 2])

        gidx = 0
        nglob = 0
        for hl in range(2):
            for g in range(NG):
                nkc = 8 * (g + 1)
                base = nglob
                for i in range(nkc):
                    kc = i
                    diag = kc >= 8 * g
                    emit_st(hl, g, kc,
                            first_flash=(True if nglob == 0 else
                                         (False if (g == 0 and kc == 0) else None)))
                    emit_exp()
                    if diag:
                        emit_mask(g, kc, nglob)
                    if i >= 2:
                        j = i - 2
                        emit_rspv(hl, g, j, j >= 8 * g, j == 0, j == nkc - 1,
                                  base + j, gidx)
                    nglob += 1
                    if nglob >= 8 and nglob % 6 == 2 and wd_steps:
                        wd_step(wd_steps.pop(0))
                for j in (nkc - 2, nkc - 1):
                    emit_rspv(hl, g, j, j >= 8 * g, j == 0, j == nkc - 1,
                              base + j, gidx)
                emit_tail(hl, g, gidx)
                gidx += 1
            if hl == 0:
                A2A0_WAIT = (sAp[0].n, sAp[1].n)

        if "o0" in dbg:
            for gg in range(NG):
                K.dma("g", [(sAp[0], sAp[0].n), (sAp[1], sAp[1].n)],
                      dbg["o0"][:, gg * QG:(gg + 1) * QG], a2a_in[0][gg], sY)
        if "o1" in dbg:
            for gg in range(NG):
                K.dma("g", [(sAp[0], sAp[0].n), (sAp[1], sAp[1].n)],
                      dbg["o1"][:, gg * QG:(gg + 1) * QG], a2a_in[1][gg], sY)

        def cc0(gq):
            with allow_ici_all_to_all():
                return gq.collective_compute(
                    "AllToAll", AluOpType.bypass,
                    replica_groups=[list(range(NC))],
                    ins=[a2a_in[0]], outs=[a2a_out[0]])

        def cc1(gq):
            with allow_ici_all_to_all():
                return gq.collective_compute(
                    "AllToAll", AluOpType.bypass,
                    replica_groups=[list(range(NC))],
                    ins=[a2a_in[1]], outs=[a2a_out[1]])
        for st in wd_steps:
            wd_step(st)
        wd_steps = []

        K.op("g", [(sAp[0], A2A0_WAIT[0]), (sAp[1], A2A0_WAIT[1])], cc0, sCc)
        K.op("g", [(sAp[0], sAp[0].n), (sAp[1], sAp[1].n)], cc1, sCc)

        fl.close()
        att.close()

        h_sb = E(nc.sbuf_tensor("h_sb", [128, ICH * 2048], BF16))

        post = es.enter_context(contextlib.ExitStack())
        EP = post.enter_context
        zt = [EP(nc.sbuf_tensor(f"zt{i}", [128, M], F32)) for i in range(2)]

        scr = EP(nc.sbuf_tensor("scr", [128, M], F32))
        scri = EP(nc.sbuf_tensor("scri", [128, M], I32))
        xqt = EP(nc.sbuf_tensor("xqt", [128, M], BF16))
        misc_ps = EP(nc.psum_tensor("misc_ps", [128, 512], F32))

        postx = es.enter_context(contextlib.ExitStack())

        wgf = [postx.enter_context(nc.sbuf_tensor(f"wgf{i}", [128, M], F32))
               for i in range(2)]
        wuf = [postx.enter_context(nc.sbuf_tensor(f"wuf{i}", [128, M], F32))
               for i in range(2)]
        wgq = postx.enter_context(nc.sbuf_tensor("wgq", [128, M], BF16))
        wuq = postx.enter_context(nc.sbuf_tensor("wuq", [128, M], BF16))
        wgT = [postx.enter_context(nc.sbuf_tensor(f"wgT{i}", [128, M], BF16)) for i in range(3)]
        wuT = [postx.enter_context(nc.sbuf_tensor(f"wuT{i}", [128, M], BF16)) for i in range(3)]
        silu_sb = postx.enter_context(nc.sbuf_tensor("silu_sb", [128, 512], BF16))
        ups_sb = postx.enter_context(nc.sbuf_tensor("ups_sb", [128, 512], BF16))
        sgc_all = postx.enter_context(nc.sbuf_tensor("sgc_all", [128, ICH], F32))
        suc_all = postx.enter_context(nc.sbuf_tensor("suc_all", [128, ICH], F32))
        wmax_all = postx.enter_context(nc.sbuf_tensor("wmax_all", [128, 2 * ICH], F32))
        tw_ps = [postx.enter_context(nc.psum_tensor(f"tw_ps{i}", [128, 1024], BF16)) for i in range(2)]

        def prepare_load(ic):
            K.dma("q", [(sWS, sWS.n)], wgf[ic % 2][:],
                  wgd[ic * 128:(ic + 1) * 128, :], sLG)
            K.dma("q", [], wuf[ic % 2][:], wud[ic * 128:(ic + 1) * 128, :], sLU)

        def prepare(ic):
            for gi, (wf_, wqq, wT) in enumerate(((wgf[ic % 2], wgq, wgT),
                                                 (wuf[ic % 2], wuq, wuT))):
                tcol = 20 + (2 * (ic % 2) + gi)

                def wqv(v, ic=ic, gi=gi, wf_=wf_, tcol=tcol):
                    v.reduce_max(sw[:, WMX:WMX + 1], wf_[:], axis=AXX,
                                 apply_absolute_value=True)
                    v.drain()
                    v.tensor_copy(wmax_all[:, gi * ICH + ic:gi * ICH + ic + 1],
                                  sw[:, WMX:WMX + 1])
                    v.reciprocal(sw[:, tcol:tcol + 1], sw[:, WMX:WMX + 1])
                    v.drain()
                    return v.tensor_scalar_mul(sw[:, tcol:tcol + 1],
                                               sw[:, tcol:tcol + 1], 127.0)
                K.op("v", [(sLG if gi == 0 else sLU, 16 * (ic + 1)),
                           (sWS, sWS.n - 1)], wqv, sWq)

                def wqs(ss, ic=ic, gi=gi, wf_=wf_, wqq=wqq, tcol=tcol):
                    ss.activation(scr[:], wf_[:], ACT.Copy, scale=sw[:, tcol:tcol + 1])
                    ss.copy(scri[:], scr[:])
                    return ss.copy(wqq[:], scri[:])
                K.op("s", [(sWq, sWq.n), (sWt, sWt.n)], wqs, sWS)

                def wtr(t, wqq=wqq, gi=gi, ic=ic):
                    for fc in range(FCH):
                        mm = t.transpose(tw_ps[gi][:, fc * 128:(fc + 1) * 128],
                                         wqq[:, fc * 128:(fc + 1) * 128], identB[:])
                    return mm
                K.op("t", [(sWS, sWS.n), (sWe, sWe.n - 1)], wtr, sWt)
                K.op("v", [(sWt, sWt.n), (sG, sG.n - 4)],
                     lambda v, wT=wT, gi=gi, ic=ic: v.tensor_copy(wT[ic % 3][:],
                                                                  tw_ps[gi][:]), sWe)

        K.op("v", [], lambda v: v.memset(sw[:, HMAX:HMAX + 1], 0.0), sH)
        K.dma("q", [(sAp[0], sAp[0].n), (sAp[1], sAp[1].n)], wgf[0][:],
              wgd[0:128, :], sLG)
        K.dma("q", [], wuf[0][:], wud[0:128, :], sLU)
        prepare_load(1)
        prepare(0)
        prepare(1)
        prepare_load(2)
        prepare_load(3)

        wop = postx.enter_context(contextlib.ExitStack())
        o_all = wop.enter_context(nc.sbuf_tensor("o_all", [128, 16 * QG], BF16))
        xr = [wop.enter_context(nc.sbuf_tensor(f"xr{i}", [128, M], F32)) for i in range(2)]
        sq_scr = wop.enter_context(nc.sbuf_tensor("sq_scr", [128, M], F32))
        wo_sb = wop.enter_context(nc.sbuf_tensor("wo_sb", [128, H * M], BF16))
        for h in range(H):
            K.dma("g", [], wo_sb[:, h * M:(h + 1) * M], wo4[h], sWO)
        n_wo = sWO.n
        attn_ps = [wop.enter_context(nc.psum_tensor(f"attn_ps{i}", [128, M], F32))
                   for i in range(2)]

        for hl in range(2):
            for j in range(NC):
                col = ((j // 4) * 8 + 2 * (j % 4) + hl) * QG
                K.dma("g", [(sCc, sCc.n - (1 if hl == 0 else 0))],
                      o_all[:, col:col + QG], a2a_out[hl][j], sOa)
        n_oall = sOa.n

        if "oall" in dbg:
            K.dma("g", [(sOa, n_oall)], dbg["oall"], o_all[:], sY)

        K.op("v", [], lambda v: v.memset(sw[:, ZMAX:ZMAX + 1], 0.0), sZv)

        for ti in range(NT):
            b_, tc = ti // 8, ti % 8
            K.dma("q", [(sZv, sZv.n - 3)], xr[ti % 2][:],
                  xres[b_, tc * 128:(tc + 1) * 128, :], sXR[ti % 2])

            def womm(t, b_=b_, tc=tc, ti=ti):
                for h in range(H):
                    col = (b_ * 8 + h) * QG + tc * 128
                    t.matmul(attn_ps[ti % 2][:, 0:512], o_all[:, col:col + 128],
                             wo_sb[:, h * M:h * M + 512],
                             start=(h == 0), stop=(h == H - 1))
                    mm = t.matmul(attn_ps[ti % 2][:, 512:M], o_all[:, col:col + 128],
                                  wo_sb[:, h * M + 512:(h + 1) * M],
                                  start=(h == 0), stop=(h == H - 1))
                return mm
            K.op("t", [(sOa, n_oall), (sWO, n_wo), (sZv, sZv.n - 3)], womm, sWo)

            def va(v, ti=ti):
                zb = zt[ti % 2]
                v.scalar_tensor_tensor(zb[:, 0:512], attn_ps[ti % 2][:, 0:512],
                                       1.0, xr[ti % 2][:, 0:512], MULT, ADD)
                v.scalar_tensor_tensor(zb[:, 512:M], attn_ps[ti % 2][:, 512:M],
                                       1.0, xr[ti % 2][:, 512:M], MULT, ADD)
                v.tensor_mul(sq_scr[:], zb[:], zb[:])
                v.reduce_sum(sw[:, MEAN:MEAN + 1], zb[:], axis=AXX)
                v.reduce_sum(sw[:, SSQ:SSQ + 1], sq_scr[:], axis=AXX)
                v.drain()
                mp = MEAN0 + ti % 2
                v.tensor_scalar_mul(sw[:, mp:mp + 1], sw[:, MEAN:MEAN + 1], 1.0 / M)
                v.drain()
                v.tensor_mul(sw[:, T1:T1 + 1], sw[:, mp:mp + 1], sw[:, mp:mp + 1])
                v.drain()
                return v.scalar_tensor_tensor(sw[:, VAR:VAR + 1],
                                              sw[:, SSQ:SSQ + 1], 1.0 / M,
                                              sw[:, T1:T1 + 1], MULT, SUB)
            K.op("v", [(sWo, sWo.n), (sXR[ti % 2], 16 * (ti // 2 + 1)),
                       (sSt, sSt.n),
                       (sZS[ti % 2], 16 * max(0, (ti - 2) // 2 + 1))], va, sZv)
            if "zpre" in dbg:
                K.dma("q", [(sZv, sZv.n)], dbg["zpre"][ti], zt[ti % 2][:], sY)
            vb_extra = [(sY, sY.n)] if "zpre" in dbg else []
            K.op("s", [(sZv, sZv.n)],
                 lambda s, ti=ti: s.activation(sw[:, STD0 + ti % 2:STD0 + ti % 2 + 1],
                                               sw[:, VAR:VAR + 1], ACT.Sqrt,
                                               bias=eps_col[:]), sSt)

            def vb(v, ti=ti):
                zb = zt[ti % 2]
                v.reciprocal(sw[:, INV:INV + 1], sw[:, STD0 + ti % 2:STD0 + ti % 2 + 1])
                v.drain()
                v.tensor_scalar(zb[:], zb[:], sw[:, MEAN0 + ti % 2:MEAN0 + ti % 2 + 1],
                                sw[:, INV:INV + 1], SUB, MULT)
                v.reduce_max(sw[:, T1:T1 + 1], zb[:], axis=AXX,
                             apply_absolute_value=True)
                v.drain()
                return v.tensor_max(sw[:, ZMAX:ZMAX + 1], sw[:, ZMAX:ZMAX + 1],
                                    sw[:, T1:T1 + 1])
            K.op("v", [(sSt, sSt.n)] + vb_extra, vb, sZv)
            K.dma("q", [(sZv, sZv.n)], z_dram[ti], zt[ti % 2][:], sZS[ti % 2])
            if "z" in dbg:
                K.dma("q", [], dbg["z"][ti], zt[ti % 2][:], sY)

        if "swd" in dbg:
            K.dma("q", [(sZv, sZv.n)], dbg["swd"], sw[:], sY)
        K.op("t", [(sZv, sZv.n)],
             lambda t: t.transpose(misc_ps[0:1, 0:128], sw[:, ZMAX:ZMAX + 1],
                                   identF[:]), sWt)
        if "mps" in dbg:
            def mpscp(v):
                return v.tensor_copy(scr[:, 0:512], misc_ps[:])
            K.op("v", [(sWt, sWt.n)], mpscp, sGl)
            K.dma("q", [(sGl, sGl.n)], dbg["mps"], scr[:, 0:512], sY)
        K.op("v", [(sWt, sWt.n)],
             lambda v: v.reduce_max(rw2[0:1, 0:1], misc_ps[0:1, 0:128], axis=AXX), sGl)
        K.dma("g", [(sGl, sGl.n)], cc_in[0:1, 0:1], rw2[0:1, 0:1], sGB)

        K.op("g", [(sGB, sGB.n)],
             lambda g: g.collective_compute("AllReduce", AMAX,
                                            replica_groups=[list(range(NC))],
                                            ins=[cc_in], outs=[cc_out]), sCc)
        K.dma("g", [(sCc, sCc.n)], rw2[0:1, 4:5], cc_out[0:1, 0:1], sGB)
        n_gm1 = sGB.n
        K.op("s", [(sC, n_const)],
             lambda s: s.activation(rw2[0:1, 5:6], clip_sb[0:1, 0:1], ACT.Sigmoid),
             sSt)

        def rx_chain(v):
            v.tensor_mul(rw2[0:1, 6:7], rw2[0:1, 4:5], rw2[0:1, 5:6])
            v.drain()
            v.reciprocal(rw2[0:1, 7:8], rw2[0:1, 6:7])
            v.drain()
            v.tensor_scalar_mul(rw2[0:1, 7:8], rw2[0:1, 7:8], 127.0)
            v.drain()
            return v.tensor_scalar_mul(rw2[0:1, 8:9], rw2[0:1, 6:7],
                                       1.0 / (127.0 * 127.0))
        K.op("v", [(sGB, n_gm1), (sSt, sSt.n)], rx_chain, sGl)

        def bc1(t):
            t.matmul(misc_ps[:, 0:1], onesF[:], rw2[0:1, 7:8], start=True, stop=True)
            return t.matmul(misc_ps[:, 1:2], onesF[:], rw2[0:1, 8:9],
                            start=True, stop=True)
        K.op("t", [(sGl, sGl.n)], bc1, sWt)

        def bc1c(v):
            v.tensor_copy(sw[:, RXC:RXC + 1], misc_ps[:, 0:1])
            return v.tensor_copy(sw[:, SXD:SXD + 1], misc_ps[:, 1:2])
        K.op("v", [(sWt, sWt.n)], bc1c, sGl)
        n_gl1 = sGl.n
        if "gmax" in dbg:
            K.dma("q", [(sGl, n_gl1)], dbg["gmax"][0:1, 0:4], rw2[0:1, 4:8], sY)

        wop.close()

        xq_fm = postx.enter_context(nc.sbuf_tensor("xq_fm", [128, FCH * 2048], BF16))
        qscope = postx.enter_context(contextlib.ExitStack())
        tq_ps = [qscope.enter_context(nc.psum_tensor(f"tq_ps{i}", [128, 1024], BF16))
                 for i in range(2)]
        for ti in range(NT):
            w = [(sXq, sXq.n - 1)]
            if ti < 2:
                w += [(sZS[0], sZS[0].n), (sZS[1], sZS[1].n)]
            K.dma("q", w, zt[ti % 2][:], z_dram[ti], sZR[ti % 2])

            def qv(v, ti=ti):
                v.tensor_scalar_mul(scr[:], zt[ti % 2][:], sw[:, RXC:RXC + 1])
                v.tensor_copy(scri[:], scr[:])
                v.tensor_scalar(scri[:], scri[:], -128, 127, AMAX, AMIN)
                return v.tensor_copy(xqt[:], scri[:])
            K.op("v", [(sZR[ti % 2], 16 * (ti // 2 + 1)), (sGl, n_gl1), (sTq, sTq.n - 1)], qv, sXq)
            for fc in range(FCH):
                K.op("t", [(sXq, sXq.n), (sDe, sDe.n - 1)],
                     lambda t, ti=ti, fc=fc: t.transpose(
                         tq_ps[fc % 2][:, 0:128], xqt[:, fc * 128:(fc + 1) * 128],
                         identB[:]), sTq)
                K.op("v", [(sTq, sTq.n)],
                     lambda v, ti=ti, fc=fc: v.tensor_copy(
                         xq_fm[:, fc * 2048 + ti * 128:fc * 2048 + (ti + 1) * 128],
                         tq_ps[fc % 2][:, 0:128]), sDe)
        n_xq = sDe.n
        if "xq" in dbg:
            K.dma("g", [(sDe, n_xq)], dbg["xq"], xq_fm[:], sY)
        qscope.close()

        ffa = postx.enter_context(contextlib.ExitStack())
        EG = ffa.enter_context
        gu_ps = [EG(nc.psum_tensor(f"gu_ps{i}", [128, 1024], F32)) for i in range(2)]

        def compute(ic):
            def sgcmul(v, ic=ic):
                v.tensor_mul(sgc_all[:, ic:ic + 1],
                             wmax_all[:, ic:ic + 1], sw[:, SXD:SXD + 1])
                return v.tensor_mul(suc_all[:, ic:ic + 1],
                                    wmax_all[:, ICH + ic:ICH + ic + 1],
                                    sw[:, SXD:SXD + 1])
            K.op("v", [(sGl, n_gl1), (sWq, 2 * (ic + 1))], sgcmul, sSg)
            for tg4 in range(4):
                def gu(t, ic=ic, tg4=tg4):
                    for fc in range(FCH):
                        t.matmul(gu_ps[tg4 % 2][:, 0:512],
                                 wgT[ic % 3][:, fc * 128:(fc + 1) * 128],
                                 xq_fm[:, fc * 2048 + tg4 * 512:fc * 2048 + (tg4 + 1) * 512],
                                 start=(fc == 0), stop=(fc == FCH - 1))
                    for fc in range(FCH):
                        mm = t.matmul(gu_ps[tg4 % 2][:, 512:1024],
                                      wuT[ic % 3][:, fc * 128:(fc + 1) * 128],
                                      xq_fm[:, fc * 2048 + tg4 * 512:fc * 2048 + (tg4 + 1) * 512],
                                      start=(fc == 0), stop=(fc == FCH - 1))
                    return mm
                K.op("t", [(sWe, sWe.n - 2 * (min(ic + 2, ICH) - 1 - ic)),
                           (sDe, n_xq), (sH, sH.n - 1)], gu, sG)
                K.op("s", [(sG, sG.n), (sH, sH.n), (sSg, sSg.n)],
                     lambda s, ic=ic, tg4=tg4: s.activation(
                         silu_sb[:], gu_ps[tg4 % 2][:, 0:512], ACT.Silu,
                         scale=sgc_all[:, ic:ic + 1]), sSi)

                K.op("s", [(sH, sH.n)],
                     lambda s, ic=ic, tg4=tg4: s.activation(
                         ups_sb[:], gu_ps[tg4 % 2][:, 512:1024], ACT.Copy,
                         scale=suc_all[:, ic:ic + 1]), sSi)

                def hv(v, ic=ic, tg4=tg4):
                    col = ic * 2048 + tg4 * 512
                    mm = v.tensor_mul(h_sb[:, col:col + 512], silu_sb[:], ups_sb[:])
                    if tg4 == 3:
                        v.reduce_max(sw[:, T1:T1 + 1],
                                     h_sb[:, ic * 2048:(ic + 1) * 2048], axis=AXX,
                                     apply_absolute_value=True)
                        v.drain()
                        mm = v.tensor_max(sw[:, HMAX:HMAX + 1],
                                          sw[:, HMAX:HMAX + 1], sw[:, T1:T1 + 1])
                    return mm
                K.op("v", [(sSi, sSi.n)], hv, sH)

        for ic in range(ICH):
            compute(ic)
            if ic + 2 < ICH:
                prepare(ic + 2)
            if ic + 4 < ICH:
                prepare_load(ic + 4)

        if "sgc" in dbg:
            def sgdump(q):
                return q.dma_start(out=dbg["sgc"][:, 0:2 * ICH], in_=wmax_all[:])
            K.op("q", [(sH, sH.n)], sgdump, sY, 16)
            K.dma("q", [], dbg["sgc"][:, 2 * ICH:3 * ICH], sgc_all[:], sY)
            K.dma("q", [], dbg["sgc"][:, 3 * ICH:4 * ICH], suc_all[:], sY)

        K.op("t", [(sH, sH.n)],
             lambda t: t.transpose(misc_ps[0:1, 0:128], sw[:, HMAX:HMAX + 1],
                                   identF[:]), sWt)
        K.op("v", [(sWt, sWt.n)],
             lambda v: v.reduce_max(rw2[0:1, 9:10], misc_ps[0:1, 0:128], axis=AXX),
             sGl)
        K.dma("g", [(sGl, sGl.n)], cc2_in[0:1, 0:1], rw2[0:1, 9:10], sGB)
        K.op("g", [(sGB, sGB.n)],
             lambda g: g.collective_compute("AllReduce", AMAX,
                                            replica_groups=[list(range(NC))],
                                            ins=[cc2_in], outs=[cc2_out]), sCc)
        K.dma("g", [(sCc, sCc.n)], rw2[0:1, 10:11], cc2_out[0:1, 0:1], sGB)
        n_gm2 = sGB.n
        K.op("s", [(sC, n_const)],
             lambda s: s.activation(rw2[0:1, 11:12], clip_sb[0:1, 1:2], ACT.Sigmoid),
             sSt)

        def rh_chain(v):
            v.tensor_mul(rw2[0:1, 12:13], rw2[0:1, 10:11], rw2[0:1, 11:12])
            v.drain()
            v.reciprocal(rw2[0:1, 13:14], rw2[0:1, 12:13])
            v.drain()
            v.tensor_scalar_mul(rw2[0:1, 13:14], rw2[0:1, 13:14], 127.0)
            v.drain()
            return v.tensor_scalar_mul(rw2[0:1, 14:15], rw2[0:1, 12:13],
                                       1.0 / (127.0 * 127.0))
        K.op("v", [(sGB, n_gm2), (sSt, sSt.n)], rh_chain, sGl)

        def bc2(t):
            t.matmul(misc_ps[:, 2:3], onesF[:], rw2[0:1, 13:14], start=True, stop=True)
            return t.matmul(misc_ps[:, 3:4], onesF[:], rw2[0:1, 14:15],
                            start=True, stop=True)
        K.op("t", [(sGl, sGl.n)], bc2, sWt)

        def bc2c(v):
            v.tensor_copy(sw[:, RHC:RHC + 1], misc_ps[:, 2:3])
            return v.tensor_copy(sw[:, SHD:SHD + 1], misc_ps[:, 3:4])
        K.op("v", [(sWt, sWt.n)], bc2c, sGl)
        n_gl2 = sGl.n
        if "gmax" in dbg:
            K.dma("q", [(sGl, n_gl2)], dbg["gmax"][0:1, 4:8], rw2[0:1, 10:14], sY)


        ffa.close()
        postx.close()

        ffd = post.enter_context(contextlib.ExitStack())
        ED = ffd.enter_context
        wdT = ED(nc.sbuf_tensor("wdT", [128, ICH * M], BF16))
        wdqD = [ED(nc.sbuf_tensor(f"wdqD{i}", [128, IDIM], BF16)) for i in range(2)]
        row_sb = ED(nc.sbuf_tensor("row_sb", [1, M], F32))
        bcast_sb = ED(nc.sbuf_tensor("bcast_sb", [128, M], BF16))
        va2scr = ED(nc.sbuf_tensor("va2scr", [128, M], F32))
        twd_ps = [ED(nc.psum_tensor(f"twd_ps{i}", [128, 128], BF16)) for i in range(2)]
        dn_ps = [ED(nc.psum_tensor(f"dn_ps{i}", [128, M], F32)) for i in range(2)]
        wdTi3 = wdT[:].rearrange("p (i m) -> p i m", i=ICH)
        n_dq = sDQ.n

        for hc in range(FCH):
            w = [(sDQ, n_dq), (sWt, sWt.n - 22)]
            if hc == 0:
                w.append((sH, sH.n))
            K.dma("q", w, wdqD[hc % 2][:], wdq_dram[hc], sDW[hc % 2])
            for ic in range(ICH):
                K.op("t", [(sDW[hc % 2], 16 * (hc // 2 + 1)), (sWe, sWe.n - 1)],
                     lambda t, hc=hc, ic=ic: t.transpose(
                         twd_ps[ic % 2][:, 0:128],
                         wdqD[hc % 2][:, ic * 128:(ic + 1) * 128], identB[:]), sWt)
                K.op("v", [(sWt, sWt.n)],
                     lambda v, hc=hc, ic=ic: v.tensor_copy(
                         wdT[:, ic * M + hc * 128:ic * M + (hc + 1) * 128],
                         twd_ps[ic % 2][:, 0:128]), sWe)

        def sdctr(t, h0):
            for hc in range(h0, h0 + 4):
                mm = t.transpose(misc_ps[0:1, (hc - h0) * 128:(hc - h0 + 1) * 128],
                                 wmaxd[:, hc:hc + 1], identF[:])
            return mm
        K.op("t", [(sWR, sWR.n), (sGl, n_gl2)],
             lambda t: sdctr(t, 0), sWt)
        K.op("v", [(sWt, sWt.n)],
             lambda v: v.tensor_copy(row_sb[0:1, 0:512], misc_ps[0:1, 0:512]), sGl)
        K.op("t", [(sGl, sGl.n)], lambda t: sdctr(t, 4), sWt)

        def sdcrow(v):
            v.tensor_copy(row_sb[0:1, 512:M], misc_ps[0:1, 0:512])
            v.drain()
            return v.tensor_scalar_mul(row_sb[0:1, :], row_sb[0:1, :],
                                       sw[0:1, SHD:SHD + 1])
        K.op("v", [(sWt, sWt.n)], sdcrow, sGl)

        def sdcbc(t):
            t.matmul(dn_ps[0][:, 0:512], onesF[:], row_sb[0:1, 0:512],
                     start=True, stop=True)
            return t.matmul(dn_ps[0][:, 512:M], onesF[:], row_sb[0:1, 512:M],
                            start=True, stop=True)
        K.op("t", [(sGl, sGl.n)], sdcbc, sWt)
        K.op("v", [(sWt, sWt.n)],
             lambda v: v.tensor_copy(bcast_sb[:], dn_ps[0][:]), sGl)
        n_bcast = sGl.n

        for ic in range(ICH):
            K.op("v", [(sGl, n_bcast)] if ic == 0 else [],
                 lambda v, ic=ic: v.tensor_mul(wdTi3[:, ic, :], wdTi3[:, ic, :],
                                               bcast_sb[:]), sWe)
        n_wdT = sWe.n
        n_wdq = sWq.n
        if "row" in dbg:
            K.dma("q", [(sGl, n_bcast)], dbg["row"], row_sb[:], sY)
        if "wmaxd" in dbg:
            K.dma("q", [(sWq, n_wdq)], dbg["wmaxd"], wmaxd[:], sY)
        if "wdt" in dbg:
            K.dma("g", [(sWe, n_wdT)], dbg["wdt"], wdT[:], sY)
        h3 = h_sb[:].rearrange("p (i t) -> p i t", i=ICH)
        HB = ((0, 8), (8, 16), (16, ICH))

        def emit_hr(ti):
            for ci, (i0, i1) in enumerate(HB):
                nel = (i1 - i0) * 128

                def hs_ap(ti=ti, i0=i0, i1=i1):
                    return h3[:, i0:i1, ti * 128:(ti + 1) * 128]

                def hmul(ss, ti=ti, i0=i0, i1=i1, nel=nel):
                    return ss.mul(scr[:, 0:nel].rearrange(
                        "p (a b) -> p a b", a=i1 - i0), hs_ap(ti, i0, i1),
                        sw[:, RHC:RHC + 1])
                w = [(sGl, n_gl2), (sWq, n_wdq)]
                if sHr.n > 0:
                    w.append((sHr, sHr.n - 0))
                K.op("s", w, hmul, sHs)

                def hcvt(v, nel=nel):
                    v.tensor_copy(scri[:, 0:nel], scr[:, 0:nel])
                    return v.tensor_scalar(scri[:, 0:nel], scri[:, 0:nel],
                                           -128, 127, AMAX, AMIN)
                K.op("v", [(sHs, sHs.n)], hcvt, sHr)

                def hback(ss, ti=ti, i0=i0, i1=i1, nel=nel):
                    return ss.copy(hs_ap(ti, i0, i1), scri[:, 0:nel].rearrange(
                        "p (a b) -> p a b", a=i1 - i0))
                K.op("s", [(sHr, sHr.n)], hback, sHs)

        emit_hr(0)
        emit_hr(1)

        for ti in range(NT):
            b_, tc = ti // 8, ti % 8
            w = [(sXq, sXq.n), (sZv, sZv.n - 4),
                 (sYP[ti % 2], 16 * max(0, (ti - 2) // 2 + 1))]
            if ti < 2:
                w += [(sZR[0], sZR[0].n), (sZR[1], sZR[1].n)]
            K.dma("q", w, zt[ti % 2][:], z_dram[ti], sZD[ti % 2])

            def dmm(t, ti=ti):
                for ic in range(ICH):
                    hcol = ic * 2048 + ti * 128
                    t.matmul(dn_ps[ti % 2][:, 0:512], h_sb[:, hcol:hcol + 128],
                             wdT[:, ic * M:ic * M + 512],
                             start=(ic == 0), stop=(ic == ICH - 1))
                    mm = t.matmul(dn_ps[ti % 2][:, 512:M], h_sb[:, hcol:hcol + 128],
                                  wdT[:, ic * M + 512:(ic + 1) * M],
                                  start=(ic == 0), stop=(ic == ICH - 1))
                return mm
            K.op("t", [(sWe, n_wdT), (sHs, 6 * (ti + 1)), (sZv, sZv.n - 3)],
                 dmm, sD)
            if ti + 2 < NT:
                emit_hr(ti + 2)

            def va2(v, ti=ti):
                zb = zt[ti % 2]
                v.scalar_tensor_tensor(zb[:, 0:512], dn_ps[ti % 2][:, 0:512],
                                       1.0, zb[:, 0:512], MULT, ADD)
                v.scalar_tensor_tensor(zb[:, 512:M], dn_ps[ti % 2][:, 512:M],
                                       1.0, zb[:, 512:M], MULT, ADD)
                v.tensor_mul(va2scr[:], zb[:], zb[:])
                v.reduce_sum(sw[:, MEAN:MEAN + 1], zb[:], axis=AXX)
                v.reduce_sum(sw[:, SSQ:SSQ + 1], va2scr[:], axis=AXX)
                v.drain()
                mp = MEAN0 + ti % 2
                v.tensor_scalar_mul(sw[:, mp:mp + 1], sw[:, MEAN:MEAN + 1], 1.0 / M)
                v.drain()
                v.tensor_mul(sw[:, T1:T1 + 1], sw[:, mp:mp + 1], sw[:, mp:mp + 1])
                v.drain()
                return v.scalar_tensor_tensor(sw[:, VAR:VAR + 1],
                                              sw[:, SSQ:SSQ + 1], 1.0 / M,
                                              sw[:, T1:T1 + 1], MULT, SUB)
            K.op("v", [(sD, sD.n), (sZD[ti % 2], 16 * (ti // 2 + 1)), (sSt, sSt.n), (sWq, n_wdq)],
                 va2, sZv)
            K.op("s", [(sZv, sZv.n)],
                 lambda s, ti=ti: s.activation(sw[:, STD0 + ti % 2:STD0 + ti % 2 + 1],
                                               sw[:, VAR:VAR + 1], ACT.Sqrt,
                                               bias=eps_col[:]), sSt)

            def vb2(v, ti=ti):
                zb = zt[ti % 2]
                v.reciprocal(sw[:, INV:INV + 1], sw[:, STD0 + ti % 2:STD0 + ti % 2 + 1])
                v.drain()
                return v.tensor_scalar(zb[:], zb[:],
                                       sw[:, MEAN0 + ti % 2:MEAN0 + ti % 2 + 1],
                                       sw[:, INV:INV + 1], SUB, MULT)
            K.op("v", [(sSt, sSt.n)], vb2, sZv)
            K.dma("q", [(sZv, sZv.n)], y_out[b_, tc * 128:(tc + 1) * 128, :],
                  zt[ti % 2][:], sYP[ti % 2])

        if "hh" in dbg:
            K.dma("g", [(sHs, sHs.n)], dbg["hh"], h_sb[:], sY)
        n_y0, n_y1, n_yd = sYP[0].n, sYP[1].n, sY.n

        def _fence(eng):
            eng.wait_ge(sYP[0].h, n_y0)
            eng.wait_ge(sYP[1].h, n_y1)
            if n_yd:
                eng.wait_ge(sY.h, n_yd)
        K.ops["q"].append(_fence)

        @block.tensor
        def _(t):
            for fn in K.ops["t"]:
                fn(t)

        @block.scalar
        def _(s):
            for fn in K.ops["s"]:
                fn(s)

        @block.vector
        def _(v):
            for fn in K.ops["v"]:
                fn(v)

        @block.sync
        def _(q):
            for fn in K.ops["q"]:
                fn(q)

        @block.gpsimd
        def _(g):
            for fn in K.ops["g"]:
                fn(g)

    return nc


_NC_CACHE = {}


def _get_nc(debug=()):
    key = tuple(sorted(debug))
    if key not in _NC_CACHE:
        _NC_CACHE[key] = build(debug=key)
    return _NC_CACHE[key]


def kernel(L0_attn_x, L0_attn_wq, L0_attn_wk, L0_attn_wv, L0_attn_wo,
           L0_ffn_clip_x, L0_ffn_clip_down, L0_ffn_w_gate, L0_ffn_w_up,
           L0_ffn_w_down, _debug=()):
    x = np.ascontiguousarray(np.asarray(L0_attn_x, dtype=np.float32))
    wq = np.ascontiguousarray(np.asarray(L0_attn_wq, dtype=np.float32))
    wk = np.ascontiguousarray(np.asarray(L0_attn_wk, dtype=np.float32))
    wv = np.ascontiguousarray(np.asarray(L0_attn_wv, dtype=np.float32))
    wo = np.ascontiguousarray(np.asarray(L0_attn_wo, dtype=np.float32))
    wg = np.ascontiguousarray(np.asarray(L0_ffn_w_gate, dtype=np.float32))
    wu = np.ascontiguousarray(np.asarray(L0_ffn_w_up, dtype=np.float32))
    wd = np.ascontiguousarray(np.asarray(L0_ffn_w_down, dtype=np.float32))
    clip2 = np.stack(
        [np.asarray(L0_ffn_clip_x, dtype=np.float32).reshape(()),
         np.asarray(L0_ffn_clip_down, dtype=np.float32).reshape(())]
    ).reshape(1, 2)

    nc_prog = _get_nc(_debug)
    in_maps = []
    for c in range(NC):
        b = c // 4
        h0 = 2 * (c % 4)
        sl = slice(c * SL, (c + 1) * SL)
        in_maps.append({
            "xb": x[b],
            "xres": np.ascontiguousarray(x[:, sl, :]),
            "wq3": np.ascontiguousarray(wq[:, h0:h0 + 2, :]),
            "wk3": np.ascontiguousarray(wk[:, h0:h0 + 2, :]),
            "wv3": np.ascontiguousarray(wv[:, h0:h0 + 2, :]),
            "wo4": wo,
            "wgd": wg,
            "wud": wu,
            "wdd": wd,
            "clips": clip2,
        })
    res = run_bass_kernel_spmd(nc_prog, in_maps, core_ids=list(range(NC)))
    y = np.empty((B, T, M), dtype=np.float32)
    for c in range(NC):
        sl = slice(c * SL, (c + 1) * SL)
        y[:, sl, :] = res.results[c]["y"]
    if _debug:
        return y, res
    return y


# version 17
# speedup vs baseline: 1.0656x; 1.0656x over previous
import contextlib
import numpy as np
import ml_dtypes
import concourse.bass as bass
import concourse.mybir as mybir
from concourse.bass_utils import run_bass_kernel_spmd
from concourse.alu_op_type import AluOpType
from concourse.aot_env import allow_ici_all_to_all

F32 = mybir.dt.float32
BF16 = mybir.dt.bfloat16
I32 = mybir.dt.int32
ACT = mybir.ActivationFunctionType
AXX = mybir.AxisListType.X
NBF = ml_dtypes.bfloat16
MULT = AluOpType.mult
ADD = AluOpType.add
SUB = AluOpType.subtract
AMAX = AluOpType.max
AMIN = AluOpType.min

B, T, M, H, D = 2, 8192, 1024, 8, 128
NC = 8
SL = T // NC
NT = 16
IDIM = 2816
ICH = 22
FCH = 8
QG = 1024
NG = T // QG
RSC = float(D) ** -0.5


class Sem:
    def __init__(self, es, nc, name):
        self.h = es.enter_context(nc.semaphore(name))
        self.n = 0


class Sched:
    ENGS = ("t", "s", "v", "q", "g")

    def __init__(self):
        self.ops = {e: [] for e in self.ENGS}

    def op(self, ek, waits, emitfn, inc=None, inck=1):
        wl = [(s.h, v) for (s, v) in waits if v > 0]
        if inc is not None:
            inc.n += inck

        def thunk(eng, wl=wl, emitfn=emitfn, inc=inc, inck=inck):
            for hh, vv in wl:
                eng.wait_ge(hh, vv)
            instr = emitfn(eng)
            if inc is not None:
                instr.then_inc(inc.h, inck)
        self.ops[ek].append(thunk)

    def dma(self, ek, waits, out, in_, inc):
        self.op(ek, waits, lambda e, o=out, i=in_: e.dma_start(out=o, in_=i),
                inc, 16)


def _tables():
    w32 = np.arange(32)
    ang = 10000.0 ** (-w32 / 32.0)
    rad = np.arange(T)[:, None] * ang[None, :]
    cosT = np.concatenate([np.cos(rad).T, np.cos(rad).T], 0).astype(NBF)
    sinT = np.concatenate([-np.sin(rad).T, np.sin(rad).T], 0).astype(NBF)
    kk = np.arange(128)[:, None]
    jj = np.arange(2 * QG)[None, :]
    band = (jj >= kk + QG).astype(NBF)
    return cosT, sinT, band


def build(debug=()):
    nc = bass.Bass("TRN2", target_bir_lowering=False, debug=False)

    xb = nc.dram_tensor("xb", [T, M], F32, kind="ExternalInput").ap()
    xres = nc.dram_tensor("xres", [B, SL, M], F32, kind="ExternalInput").ap()
    wq3 = nc.dram_tensor("wq3", [M, 2, D], F32, kind="ExternalInput").ap()
    wk3 = nc.dram_tensor("wk3", [M, 2, D], F32, kind="ExternalInput").ap()
    wv3 = nc.dram_tensor("wv3", [M, 2, D], F32, kind="ExternalInput").ap()
    wo4 = nc.dram_tensor("wo4", [H, D, M], F32, kind="ExternalInput").ap()
    wgd = nc.dram_tensor("wgd", [IDIM, M], F32, kind="ExternalInput").ap()
    wud = nc.dram_tensor("wud", [IDIM, M], F32, kind="ExternalInput").ap()
    wdd = nc.dram_tensor("wdd", [M, IDIM], F32, kind="ExternalInput").ap()
    clips = nc.dram_tensor("clips", [1, 2], F32, kind="ExternalInput").ap()
    y_out = nc.dram_tensor("y", [B, SL, M], F32, kind="ExternalOutput").ap()

    a2a_in = [nc.dram_tensor(f"a2a_in{hl}", [NC, 128, QG], BF16, kind="Internal").ap()
              for hl in range(2)]
    a2a_out = [nc.dram_tensor(f"a2a_out{hl}", [NC, 128, QG], BF16, kind="Internal").ap()
               for hl in range(2)]
    cc_in = nc.dram_tensor("cc_in", [1, 1], F32, kind="Internal").ap()
    cc_out = nc.dram_tensor("cc_out", [1, 1], F32, kind="Internal").ap()
    cc2_in = nc.dram_tensor("cc2_in", [1, 1], F32, kind="Internal").ap()
    cc2_out = nc.dram_tensor("cc2_out", [1, 1], F32, kind="Internal").ap()
    z_dram = nc.dram_tensor("z_dram", [NT, 128, M], F32, kind="Internal").ap()
    wdq_dram = nc.dram_tensor("wdq_dram", [FCH, 128, IDIM], BF16, kind="Internal").ap()

    cosT_np, sinT_np, band_np = _tables()
    c_cos = nc.inline_tensor(cosT_np, "c_cos").ap()
    c_sin = nc.inline_tensor(sinT_np, "c_sin").ap()
    c_band = nc.inline_tensor(band_np, "c_band").ap()
    c_identB = nc.inline_tensor(np.eye(128, dtype=NBF), "c_identB").ap()
    c_identF = nc.inline_tensor(np.eye(128, dtype=np.float32), "c_identF").ap()
    c_onesB = nc.inline_tensor(np.ones((128, 1), dtype=NBF), "c_onesB").ap()
    c_onesF = nc.inline_tensor(np.ones((1, 128), dtype=np.float32), "c_onesF").ap()
    c_eps = nc.inline_tensor(np.full((128, 1), 1e-5, dtype=np.float32), "c_eps").ap()

    dbg = {}
    for name, shape in (("z", [NT, 128, M]), ("zpre", [NT, 128, M]),
                        ("sgc", [128, 4 * ICH]),
                        ("swd", [128, 32]), ("mps", [128, 512]),
                        ("row", [1, 1024]), ("wmaxd", [128, 8]),
                        ("wdt", [128, ICH * 1024]),
                        ("hh", [128, ICH * 2048]),
                        ("gmax", [1, 8]), ("q0", [128, T]), ("o0", [128, T]),
                        ("q1", [128, T]), ("k1", [128, T]), ("o1", [128, T]),
                        ("oall", [128, 16 * QG]), ("xq", [128, FCH * 2048])):
        if name in debug:
            dbg[name] = nc.dram_tensor("dbg_" + name, shape, F32,
                                       kind="ExternalOutput").ap()

    K = Sched()
    es = contextlib.ExitStack()
    with es:
        E = es.enter_context
        identB = E(nc.sbuf_tensor("identB", [128, 128], BF16))
        identF = E(nc.sbuf_tensor("identF", [128, 128], F32))
        onesB = E(nc.sbuf_tensor("onesB", [128, 1], BF16))
        onesF = E(nc.sbuf_tensor("onesF", [1, 128], F32))

        sw = E(nc.sbuf_tensor("sw", [128, 32], F32))
        rw2 = E(nc.sbuf_tensor("rw2", [1, 64], F32))
        clip_sb = E(nc.sbuf_tensor("clip_sb", [1, 2], F32))
        eps_col = E(nc.sbuf_tensor("eps_col", [128, 1], F32))
        wmaxd = E(nc.sbuf_tensor("wmaxd", [128, 8], F32))

        MEAN, SSQ, VAR, T1, INV = 0, 1, 2, 3, 4
        ZMAX, RXC, SXD, WMX, SGC, SUC = 5, 6, 7, 8, 9, 10
        HMAX, RHC, SHD, SDC = 11, 12, 13, 14
        MEAN0, STD0 = 16, 18

        S = {}
        for name in ("sC sX sW sT sE sQ sQE sVE sSw sR sM sEx sMk sPv sRc "
                     "sNm sA sCc sOa sWo sZv sSt sZs sXq sTq sDe sG sSi sH "
                     "sWq sWt sWe sD sY sHr sGl sXC sHs sWS sSg "
                     "sWX0 sWX1 sWX2 sXR0 sXR1 sZS0 sZS1 sZR0 sZR1 sZD0 sZD1 "
                     "sLG sLU sDF0 sDF1 sWO sGB sA0 sA1 sTB sYP0 sYP1 sDQ sDW0 sDW1 sWR").split():
            S[name] = Sem(es, nc, name)
        sC, sX, sW, sT, sE = S["sC"], S["sX"], S["sW"], S["sT"], S["sE"]
        sQ, sQE, sVE, sSw, sR = S["sQ"], S["sQE"], S["sVE"], S["sSw"], S["sR"]
        sM, sEx, sMk, sPv, sRc = S["sM"], S["sEx"], S["sMk"], S["sPv"], S["sRc"]
        sNm, sA, sCc, sOa, sWo = S["sNm"], S["sA"], S["sCc"], S["sOa"], S["sWo"]
        sZv, sSt, sZs, sXq, sTq = S["sZv"], S["sSt"], S["sZs"], S["sXq"], S["sTq"]
        sDe, sG, sSi, sH, sWq = S["sDe"], S["sG"], S["sSi"], S["sH"], S["sWq"]
        sWt, sWe, sD, sY, sHr, sGl = (S["sWt"], S["sWe"], S["sD"], S["sY"],
                                      S["sHr"], S["sGl"])
        sXC = S["sXC"]
        sHs = S["sHs"]
        sWS = S["sWS"]
        sSg = S["sSg"]
        sWX = [S["sWX0"], S["sWX1"], S["sWX2"]]
        sXR = [S["sXR0"], S["sXR1"]]
        sZS = [S["sZS0"], S["sZS1"]]
        sZR = [S["sZR0"], S["sZR1"]]
        sZD = [S["sZD0"], S["sZD1"]]
        sLG, sLU = S["sLG"], S["sLU"]
        sDF = [S["sDF0"], S["sDF1"]]
        sWO, sGB, sTB = S["sWO"], S["sGB"], S["sTB"]
        sAp = [S["sA0"], S["sA1"]]
        sYP = [S["sYP0"], S["sYP1"]]
        sDQ = S["sDQ"]
        sDW = [S["sDW0"], S["sDW1"]]
        sWR = S["sWR"]

        block = E(nc.Block())

        for dst, src in ((identB[:], c_identB), (identF[:], c_identF),
                         (onesB[:], c_onesB), (onesF[:], c_onesF),
                         (clip_sb[:], clips),
                         (eps_col[:], c_eps)):
            K.dma("q", [], dst, src, sC)

        att = es.enter_context(contextlib.ExitStack())
        EA = att.enter_context
        band = EA(nc.sbuf_tensor("band", [128, 2 * QG], BF16))
        rw = EA(nc.sbuf_tensor("rw", [1, 2 * QG], F32))
        cosT = EA(nc.sbuf_tensor("cosT", [64, T // 4], BF16))
        sinT = EA(nc.sbuf_tensor("sinT", [64, T // 4], BF16))
        wq_sb = EA(nc.sbuf_tensor("wq_sb", [128, 2 * FCH * D], BF16))
        wk_sb = EA(nc.sbuf_tensor("wk_sb", [128, 2 * FCH * D], BF16))
        wv_mv = EA(nc.sbuf_tensor("wv_mv", [128, FCH * 2 * D], BF16))
        QF = [EA(nc.sbuf_tensor(f"QF{h}", [128, T], BF16)) for h in range(2)]
        KF = [EA(nc.sbuf_tensor(f"KF{h}", [128, T], BF16)) for h in range(2)]
        VT = [EA(nc.sbuf_tensor(f"VT{h}", [128, T], BF16)) for h in range(2)]
        swp = EA(nc.sbuf_tensor("swp", [64, T // 4], BF16))
        tmp2 = EA(nc.sbuf_tensor("tmp2", [64, T // 4], BF16))
        pt = [EA(nc.sbuf_tensor(f"pt{i}", [128, QG], BF16)) for i in range(3)]
        obuf = [EA(nc.sbuf_tensor(f"obuf{i}", [128, QG], BF16)) for i in range(2)]
        rbc = EA(nc.sbuf_tensor("rbc", [128, QG], BF16))

        K.dma("q", [], band[:], c_band, sC)
        n_const = sC.n

        for h in range(2):
            for fc in range(FCH):
                K.dma("g", [], wq_sb[:, (h * FCH + fc) * D:(h * FCH + fc + 1) * D],
                      wq3[fc * 128:(fc + 1) * 128, h, :], sX)
                K.dma("g", [], wk_sb[:, (h * FCH + fc) * D:(h * FCH + fc + 1) * D],
                      wk3[fc * 128:(fc + 1) * 128, h, :], sX)
        for fc in range(FCH):
            K.dma("g", [], wv_mv[:, fc * 256:(fc + 1) * 256],
                  wv3[fc * 128:(fc + 1) * 128, :, :], sX)
        n_wqkv = sX.n

        xp = att.enter_context(contextlib.ExitStack())
        EX = xp.enter_context
        xtm = [EX(nc.sbuf_tensor(f"xtm{i}", [128, 4096], BF16)) for i in range(2)]
        xf32 = [EX(nc.sbuf_tensor(f"xf32{i}", [128, 1024], F32)) for i in range(2)]
        xfm = [EX(nc.sbuf_tensor(f"xfm{i}", [128, FCH * 512], BF16)) for i in range(2)]
        xt_ps = [EX(nc.psum_tensor(f"xt_ps{i}", [128, 1024], BF16)) for i in range(2)]
        q_ps = [EX(nc.psum_tensor(f"q_ps{h}", [128, 512], F32)) for h in range(2)]
        k_ps = [EX(nc.psum_tensor(f"k_ps{h}", [128, 512], F32)) for h in range(2)]
        v_ps = [EX(nc.psum_tensor(f"v_ps{i}", [128, 512], F32)) for i in range(2)]

        TH = T // 4
        rope_done = {}

        def emit_rope_pair(qt, pr):
            cs = slice(qt * TH, (qt + 1) * TH)
            evac_need = 16 * (qt + 1)
            TTl, h = ((0, 0), (1, 0), (0, 1), (1, 1))[pr]
            TT = (QF, KF)[TTl]
            if pr == 0:
                wtb = [] if sR.n == 0 else [(sR, sR.n - 1)]
                K.dma("g", wtb, cosT[:], c_cos[:, cs], sTB)
                K.dma("g", [], sinT[:], c_sin[:, cs], sTB)
            ntab = sTB.n
            w0 = [(sQE, evac_need)] if sR.n == 0 else [(sR, sR.n - 2),
                                                       (sQE, evac_need)]
            K.dma("g", w0, swp[0:32, :], TT[h][32:64, cs], sSw)
            K.dma("g", [], swp[32:64, :], TT[h][0:32, cs], sSw)
            K.op("v", [(sSw, sSw.n), (sTB, ntab)],
                 lambda v: v.tensor_mul(tmp2[:], swp[:], sinT[:]), sR)
            K.op("v", [],
                 lambda v, TT=TT, h=h, cs=cs: v.tensor_mul(
                     TT[h][0:64, cs], TT[h][0:64, cs], cosT[:]), sR)
            K.op("v", [],
                 lambda v, TT=TT, h=h, cs=cs: v.tensor_add(
                     TT[h][0:64, cs], TT[h][0:64, cs], tmp2[:]), sR)
            rope_done[(TTl, h)] = max(rope_done.get((TTl, h), 0), sR.n)

        for tg in range(16):
            for t4 in range(4):
                ii = tg * 4 + t4
                r0 = tg * 512 + t4 * 128
                K.dma("q", [(sXC, sXC.n - 1)],
                      xf32[ii % 2][:], xb[r0:r0 + 128, :], sWX[ii % 2])
                K.op("s", [(sWX[ii % 2], 16 * (ii // 2 + 1)), (sT, sT.n - 8)],
                     lambda ss, tg=tg, t4=t4, ii=ii: ss.copy(
                         xtm[tg % 2][:, t4 * 1024:(t4 + 1) * 1024], xf32[ii % 2][:]),
                     sXC)
            for fc in range(FCH):
                def tr(t, tg=tg, fc=fc):
                    for t4 in range(4):
                        mm = t.transpose(
                            xt_ps[fc % 2][:, t4 * 128:(t4 + 1) * 128],
                            xtm[tg % 2][:, t4 * 1024 + fc * 128:t4 * 1024 + (fc + 1) * 128],
                            identB[:])
                    return mm
                K.op("t", [(sXC, sXC.n), (sE, sE.n - 1), (sC, n_const)], tr, sT)
                K.op("v", [(sT, sT.n), (sQ, sQ.n - 6)],
                     lambda v, tg=tg, fc=fc: v.tensor_copy(
                         xfm[tg % 2][:, fc * 512:(fc + 1) * 512], xt_ps[fc % 2][:, 0:512]),
                     sE)
            for h in range(2):
                def qk(t, tg=tg, h=h):
                    for fc in range(FCH):
                        t.matmul(q_ps[h][:],
                                 wq_sb[:, (h * FCH + fc) * D:(h * FCH + fc + 1) * D],
                                 xfm[tg % 2][:, fc * 512:(fc + 1) * 512],
                                 start=(fc == 0), stop=(fc == FCH - 1))
                    for fc in range(FCH):
                        mm = t.matmul(k_ps[h][:],
                                      wk_sb[:, (h * FCH + fc) * D:(h * FCH + fc + 1) * D],
                                      xfm[tg % 2][:, fc * 512:(fc + 1) * 512],
                                      start=(fc == 0), stop=(fc == FCH - 1))
                    return mm
                K.op("t", [(sE, sE.n), (sQE, sQE.n - 2), (sX, n_wqkv)], qk, sQ)
                K.op("s", [(sQ, sQ.n)],
                     lambda s, tg=tg, h=h: s.mul(
                         QF[h][:, tg * 512:(tg + 1) * 512], q_ps[h][:], RSC), sQE)
                K.op("s", [],
                     lambda s, tg=tg, h=h: s.mul(
                         KF[h][:, tg * 512:(tg + 1) * 512], k_ps[h][:], RSC), sQE)
            for tch in range(4):
                def vv(t, tg=tg, tch=tch):
                    for fc in range(FCH):
                        mm = t.matmul(
                            v_ps[tch % 2][:, 0:256],
                            xfm[tg % 2][:, fc * 512 + tch * 128:fc * 512 + (tch + 1) * 128],
                            wv_mv[:, fc * 256:(fc + 1) * 256],
                            start=(fc == 0), stop=(fc == FCH - 1))
                    return mm
                K.op("t", [(sVE, sVE.n - 1)], vv, sQ)

                def vev(v, tg=tg, tch=tch):
                    for h in range(2):
                        mm = v.tensor_copy(
                            VT[h][:, (tg * 4 + tch) * 128:(tg * 4 + tch + 1) * 128],
                            v_ps[tch % 2][:, h * 128:(h + 1) * 128])
                    return mm
                K.op("v", [(sQ, sQ.n)], vev, sVE)
            for qt in range(4):
                pr = tg - (4 * qt + 3)
                if 0 <= pr <= 3:
                    emit_rope_pair(qt, pr)

        for pr in range(1, 4):
            emit_rope_pair(3, pr)

        n_qkdone = sQE.n
        n_ve = sVE.n
        xp.close()

        n_rope = sR.n
        rope_unit = {0: rope_done[(1, 0)], 1: n_rope}

        if "q0" in dbg:
            K.dma("g", [(sR, n_rope)], dbg["q0"], QF[0][:], sY)
        if "q1" in dbg:
            K.dma("g", [(sR, n_rope)], dbg["q1"], QF[1][:], sY)
        if "k1" in dbg:
            K.dma("g", [(sR, n_rope)], dbg["k1"], KF[1][:], sY)

        fl = att.enter_context(contextlib.ExitStack())
        EF = fl.enter_context
        st_ps = [EF(nc.psum_tensor(f"st_ps{i}", [128, QG], F32)) for i in range(2)]
        o_ps = EF(nc.psum_tensor("o_ps", [128, QG], F32))
        rsA = EF(nc.psum_tensor("rsA", [128, 512], F32))
        rsB = EF(nc.psum_tensor("rsB", [128, 512], F32))
        wdl3 = [EF(nc.sbuf_tensor(f"wdl3{i}", [128, 1024], F32)) for i in range(3)]
        wscr = EF(nc.sbuf_tensor("wscr", [128, 1024], F32))
        wscri = EF(nc.sbuf_tensor("wscri", [128, 1024], I32))
        wdqC = [EF(nc.sbuf_tensor(f"wdqC{i}", [128, 1024], BF16)) for i in range(2)]

        WD_CH = ((0, 1024), (1024, 2048), (2048, IDIM))
        base_ws, base_df, base_wr = sWS.n, sDF[0].n, sWR.n
        wd_steps = []
        for hc_ in range(FCH):
            wd_steps += [("L", hc_, ci) for ci in range(3)]
            wd_steps += [("R", hc_, ci) for ci in range(3)]
            wd_steps += [("W", hc_, 0)]
            wd_steps += [("S", hc_, ci) for ci in range(3)]

        def wd_step(st):
            kind, hc, ci = st
            c0, c1 = WD_CH[ci]
            wdt = c1 - c0
            if kind == "L":
                w = []
                if hc == 0 and ci == 0:
                    w = [(sQE, n_qkdone), (sVE, n_ve)]
                elif hc > 0:
                    w = [(sWS, base_ws + 3 * (hc - 1) + ci + 1)]
                K.dma("q", w, wdl3[ci][:, 0:wdt],
                      wdd[hc * 128:(hc + 1) * 128, c0:c1], sDF[0])
            elif kind == "R":
                def wred(v, hc=hc, ci=ci, wdt=wdt):
                    v.reduce_max(sw[:, 25:26], wdl3[ci][:, 0:wdt],
                                 axis=AXX, apply_absolute_value=True)
                    v.drain()
                    if ci == 0:
                        return v.tensor_copy(wmaxd[:, hc:hc + 1], sw[:, 25:26])
                    return v.tensor_max(wmaxd[:, hc:hc + 1],
                                        wmaxd[:, hc:hc + 1], sw[:, 25:26])
                K.op("v", [(sDF[0], base_df + 16 * (3 * hc + ci + 1))], wred, sWR)
            elif kind == "W":
                def wrecip(v, hc=hc):
                    v.drain()
                    v.reciprocal(sw[:, 24:25], wmaxd[:, hc:hc + 1])
                    v.drain()
                    return v.tensor_scalar_mul(sw[:, 24:25], sw[:, 24:25], 127.0)
                w = [(sWS, base_ws + 3 * hc)] if hc > 0 else []
                K.op("v", w, wrecip, sWR)
            else:
                def wds_(ss, hc=hc, ci=ci, wdt=wdt):
                    ss.activation(wscr[:, 0:wdt], wdl3[ci][:, 0:wdt],
                                  ACT.Copy, scale=sw[:, 24:25])
                    ss.copy(wscri[:, 0:wdt], wscr[:, 0:wdt])
                    return ss.copy(wdqC[ci % 2][:, 0:wdt], wscri[:, 0:wdt])
                K.op("s", [(sWR, base_wr + 4 * (hc + 1)),
                           (sDQ, sDQ.n - 16)], wds_, sWS)
                K.dma("q", [(sWS, sWS.n)], wdq_dram[hc, :, c0:c1],
                      wdqC[ci % 2][:, 0:wdt], sDQ)

        mask_cnt_after = {}

        def emit_st(hl, g, kc, first_flash):
            nn = sM.n

            def st(t, hl=hl, g=g, kc=kc, nn=nn):
                ks = kc * 128
                t.matmul(st_ps[nn % 2][:, 0:512], KF[hl][:, ks:ks + 128],
                         QF[hl][:, g * QG:g * QG + 512], start=True, stop=True)
                return t.matmul(st_ps[nn % 2][:, 512:QG], KF[hl][:, ks:ks + 128],
                                QF[hl][:, g * QG + 512:(g + 1) * QG],
                                start=True, stop=True)
            w = [(sEx, sEx.n - 1)]
            if first_flash is not None:
                w += [(sR, rope_unit[hl])]
                if first_flash:
                    w.append((sVE, n_ve))
                    w.append((sQE, n_qkdone))
            K.op("t", w, st, sM)

        def emit_exp():
            nn = sEx.n
            K.op("s", [(sM, sM.n), (sPv, sPv.n)],
                 lambda s, nn=nn: s.activation(pt[nn % 3][:], st_ps[nn % 2][:],
                                               ACT.Exp), sEx)

        def emit_mask(g, kc, nglob):
            off = QG - (kc * 128 - g * QG)
            w = [(sEx, sEx.n)] + ([(sC, n_const)] if nglob == 0 else [])
            K.op("v", w,
                 lambda v, nn=nglob, off=off: v.tensor_mul(
                     pt[nn % 3][:], pt[nn % 3][:], band[:, off:off + QG]), sMk)
            mask_cnt_after[nglob] = sMk.n

        def emit_rspv(hl, g, kc, diag, first, last, nglob, gidx):
            if diag:
                w = [(sMk, mask_cnt_after[nglob])]
            else:
                w = [(sEx, nglob + 1)]
            if first and gidx > 0:
                w += [(sNm, sNm.n), (sRc, sRc.n)]

            def rp(t, hl=hl, kc=kc, nn=nglob, first=first, last=last):
                pp = pt[nn % 3]
                ks = kc * 128
                t.matmul(rsA[0:1, 0:512], onesB[:], pp[:, 0:512],
                         start=first, stop=last)
                t.matmul(rsB[0:1, 0:512], onesB[:], pp[:, 512:QG],
                         start=first, stop=last)
                t.matmul(o_ps[:, 0:512], VT[hl][:, ks:ks + 128],
                         pp[:, 0:512], start=first, stop=last)
                return t.matmul(o_ps[:, 512:QG], VT[hl][:, ks:ks + 128],
                                pp[:, 512:QG], start=first, stop=last)
            K.op("t", w, rp, sPv)

        def emit_tail(hl, g, gidx):
            def v1(v):
                v.tensor_copy(rw[0:1, 0:512], rsA[0:1, 0:512])
                return v.tensor_copy(rw[0:1, 512:QG], rsB[0:1, 0:512])
            K.op("v", [(sPv, sPv.n)], v1, sRc)

            def bc(t):
                t.matmul(rsA[:, 0:512], onesF[:], rw[0:1, 0:512],
                         start=True, stop=True)
                return t.matmul(rsB[:, 0:512], onesF[:], rw[0:1, 512:QG],
                                start=True, stop=True)
            K.op("t", [(sRc, sRc.n)], bc, sRc)

            def v2(v):
                with nc.allow_low_precision(reason="bf16 recip == old f32 recip + bf16 copy"):
                    v.reciprocal(rbc[:, 0:512], rsA[:])
                    return v.reciprocal(rbc[:, 512:QG], rsB[:])
            K.op("v", [(sRc, sRc.n)], v2, sRc)
            K.op("v", [(sAp[gidx % 2], 16 * max(0, (gidx - 2) // 2 + 1))],
                 lambda v, gidx=gidx: v.tensor_mul(obuf[gidx % 2][:], o_ps[:],
                                                   rbc[:]), sNm)
            K.dma("g", [(sNm, sNm.n)], a2a_in[hl][g], obuf[gidx % 2][:],
                  sAp[gidx % 2])

        gidx = 0
        nglob = 0
        for hl in range(2):
            for g in range(NG):
                nkc = 8 * (g + 1)
                base = nglob
                for i in range(nkc):
                    kc = i
                    diag = kc >= 8 * g
                    emit_st(hl, g, kc,
                            first_flash=(True if nglob == 0 else
                                         (False if (g == 0 and kc == 0) else None)))
                    emit_exp()
                    if diag:
                        emit_mask(g, kc, nglob)
                    if i >= 2:
                        j = i - 2
                        emit_rspv(hl, g, j, j >= 8 * g, j == 0, j == nkc - 1,
                                  base + j, gidx)
                    nglob += 1
                    if nglob >= 8 and nglob % 6 == 2 and wd_steps:
                        wd_step(wd_steps.pop(0))
                for j in (nkc - 2, nkc - 1):
                    emit_rspv(hl, g, j, j >= 8 * g, j == 0, j == nkc - 1,
                              base + j, gidx)
                emit_tail(hl, g, gidx)
                gidx += 1
            if hl == 0:
                A2A0_WAIT = (sAp[0].n, sAp[1].n)

        if "o0" in dbg:
            for gg in range(NG):
                K.dma("g", [(sAp[0], sAp[0].n), (sAp[1], sAp[1].n)],
                      dbg["o0"][:, gg * QG:(gg + 1) * QG], a2a_in[0][gg], sY)
        if "o1" in dbg:
            for gg in range(NG):
                K.dma("g", [(sAp[0], sAp[0].n), (sAp[1], sAp[1].n)],
                      dbg["o1"][:, gg * QG:(gg + 1) * QG], a2a_in[1][gg], sY)

        def cc0(gq):
            with allow_ici_all_to_all():
                return gq.collective_compute(
                    "AllToAll", AluOpType.bypass,
                    replica_groups=[list(range(NC))],
                    ins=[a2a_in[0]], outs=[a2a_out[0]])

        def cc1(gq):
            with allow_ici_all_to_all():
                return gq.collective_compute(
                    "AllToAll", AluOpType.bypass,
                    replica_groups=[list(range(NC))],
                    ins=[a2a_in[1]], outs=[a2a_out[1]])
        for st in wd_steps:
            wd_step(st)
        wd_steps = []

        K.op("g", [(sAp[0], A2A0_WAIT[0]), (sAp[1], A2A0_WAIT[1])], cc0, sCc)
        K.op("g", [(sAp[0], sAp[0].n), (sAp[1], sAp[1].n)], cc1, sCc)

        fl.close()
        att.close()

        h_sb = E(nc.sbuf_tensor("h_sb", [128, ICH * 2048], BF16))

        post = es.enter_context(contextlib.ExitStack())
        EP = post.enter_context
        zt = [EP(nc.sbuf_tensor(f"zt{i}", [128, M], F32)) for i in range(2)]

        scr = EP(nc.sbuf_tensor("scr", [128, M], F32))
        scri = EP(nc.sbuf_tensor("scri", [128, M], I32))
        xqt = EP(nc.sbuf_tensor("xqt", [128, M], BF16))
        misc_ps = EP(nc.psum_tensor("misc_ps", [128, 512], F32))

        postx = es.enter_context(contextlib.ExitStack())

        wgf = [postx.enter_context(nc.sbuf_tensor(f"wgf{i}", [128, M], F32))
               for i in range(2)]
        wuf = [postx.enter_context(nc.sbuf_tensor(f"wuf{i}", [128, M], F32))
               for i in range(2)]
        wgq = postx.enter_context(nc.sbuf_tensor("wgq", [128, M], BF16))
        wuq = postx.enter_context(nc.sbuf_tensor("wuq", [128, M], BF16))
        wgT = [postx.enter_context(nc.sbuf_tensor(f"wgT{i}", [128, M], BF16)) for i in range(3)]
        wuT = [postx.enter_context(nc.sbuf_tensor(f"wuT{i}", [128, M], BF16)) for i in range(3)]
        silu_sb = postx.enter_context(nc.sbuf_tensor("silu_sb", [128, 512], BF16))
        ups_sb = postx.enter_context(nc.sbuf_tensor("ups_sb", [128, 512], BF16))
        sgc_all = postx.enter_context(nc.sbuf_tensor("sgc_all", [128, ICH], F32))
        suc_all = postx.enter_context(nc.sbuf_tensor("suc_all", [128, ICH], F32))
        wmax_all = postx.enter_context(nc.sbuf_tensor("wmax_all", [128, 2 * ICH], F32))
        tw_ps = [postx.enter_context(nc.psum_tensor(f"tw_ps{i}", [128, 1024], BF16)) for i in range(2)]

        def prepare_load(ic):
            K.dma("q", [(sWS, sWS.n)], wgf[ic % 2][:],
                  wgd[ic * 128:(ic + 1) * 128, :], sLG)
            K.dma("q", [], wuf[ic % 2][:], wud[ic * 128:(ic + 1) * 128, :], sLU)

        def prepare_vs(ic):
            for gi, (wf_, wqq) in enumerate(((wgf[ic % 2], wgq),
                                             (wuf[ic % 2], wuq))):
                tcol = 20 + (2 * (ic % 2) + gi)
                wcol = gi * ICH + ic

                def wqv(v, ic=ic, gi=gi, wf_=wf_, tcol=tcol, wcol=wcol):
                    v.reduce_max(wmax_all[:, wcol:wcol + 1], wf_[:], axis=AXX,
                                 apply_absolute_value=True)
                    v.drain()
                    v.reciprocal(sw[:, tcol:tcol + 1],
                                 wmax_all[:, wcol:wcol + 1])
                    v.drain()
                    return v.tensor_scalar_mul(sw[:, tcol:tcol + 1],
                                               sw[:, tcol:tcol + 1], 127.0)
                K.op("v", [(sLG if gi == 0 else sLU, 16 * (ic + 1)),
                           (sWS, sWS.n - 1)], wqv, sWq)

                def wqs(ss, ic=ic, gi=gi, wf_=wf_, wqq=wqq, tcol=tcol):
                    ss.activation(scr[:], wf_[:], ACT.Copy, scale=sw[:, tcol:tcol + 1])
                    ss.copy(scri[:], scr[:])
                    return ss.copy(wqq[:], scri[:])
                K.op("s", [(sWq, sWq.n), (sWt, sWt.n)], wqs, sWS)

        def prepare_tr(ic):
            for gi, (wqq, wT) in enumerate(((wgq, wgT), (wuq, wuT))):
                def wtr(t, wqq=wqq, gi=gi, ic=ic):
                    for fc in range(FCH):
                        mm = t.transpose(tw_ps[gi][:, fc * 128:(fc + 1) * 128],
                                         wqq[:, fc * 128:(fc + 1) * 128], identB[:])
                    return mm
                K.op("t", [(sWS, sWS.n), (sWe, sWe.n - 1)], wtr, sWt)
                K.op("v", [(sWt, sWt.n), (sG, sG.n - 4)],
                     lambda v, wT=wT, gi=gi, ic=ic: v.tensor_copy(wT[ic % 3][:],
                                                                  tw_ps[gi][:]), sWe)

        def prepare(ic):
            prepare_vs(ic)
            prepare_tr(ic)

        K.op("v", [], lambda v: v.memset(sw[:, HMAX:HMAX + 1], 0.0), sH)
        K.dma("q", [(sAp[0], sAp[0].n), (sAp[1], sAp[1].n)], wgf[0][:],
              wgd[0:128, :], sLG)
        K.dma("q", [], wuf[0][:], wud[0:128, :], sLU)
        prepare_load(1)
        prepare(0)
        prepare(1)
        prepare_load(2)
        prepare_load(3)

        wop = postx.enter_context(contextlib.ExitStack())
        o_all = wop.enter_context(nc.sbuf_tensor("o_all", [128, 16 * QG], BF16))
        xr = [wop.enter_context(nc.sbuf_tensor(f"xr{i}", [128, M], F32)) for i in range(2)]
        sq_scr = wop.enter_context(nc.sbuf_tensor("sq_scr", [128, M], F32))
        wo_sb = wop.enter_context(nc.sbuf_tensor("wo_sb", [128, H * M], BF16))
        for h in range(H):
            K.dma("g", [], wo_sb[:, h * M:(h + 1) * M], wo4[h], sWO)
        n_wo = sWO.n
        attn_ps = [wop.enter_context(nc.psum_tensor(f"attn_ps{i}", [128, M], F32))
                   for i in range(2)]

        for hl in range(2):
            for j in range(NC):
                col = ((j // 4) * 8 + 2 * (j % 4) + hl) * QG
                K.dma("g", [(sCc, sCc.n - (1 if hl == 0 else 0))],
                      o_all[:, col:col + QG], a2a_out[hl][j], sOa)
        n_oall = sOa.n

        if "oall" in dbg:
            K.dma("g", [(sOa, n_oall)], dbg["oall"], o_all[:], sY)

        K.op("v", [], lambda v: v.memset(sw[:, ZMAX:ZMAX + 1], 0.0), sZv)

        for ti in range(NT):
            b_, tc = ti // 8, ti % 8
            K.dma("q", [(sZv, sZv.n - 3)], xr[ti % 2][:],
                  xres[b_, tc * 128:(tc + 1) * 128, :], sXR[ti % 2])

            def womm(t, b_=b_, tc=tc, ti=ti):
                for h in range(H):
                    col = (b_ * 8 + h) * QG + tc * 128
                    t.matmul(attn_ps[ti % 2][:, 0:512], o_all[:, col:col + 128],
                             wo_sb[:, h * M:h * M + 512],
                             start=(h == 0), stop=(h == H - 1))
                    mm = t.matmul(attn_ps[ti % 2][:, 512:M], o_all[:, col:col + 128],
                                  wo_sb[:, h * M + 512:(h + 1) * M],
                                  start=(h == 0), stop=(h == H - 1))
                return mm
            K.op("t", [(sOa, n_oall), (sWO, n_wo), (sZv, sZv.n - 3)], womm, sWo)

            def va(v, ti=ti):
                zb = zt[ti % 2]
                v.scalar_tensor_tensor(zb[:, 0:512], attn_ps[ti % 2][:, 0:512],
                                       1.0, xr[ti % 2][:, 0:512], MULT, ADD)
                v.scalar_tensor_tensor(zb[:, 512:M], attn_ps[ti % 2][:, 512:M],
                                       1.0, xr[ti % 2][:, 512:M], MULT, ADD)
                v.tensor_mul(sq_scr[:], zb[:], zb[:])
                v.reduce_sum(sw[:, MEAN:MEAN + 1], zb[:], axis=AXX)
                v.reduce_sum(sw[:, SSQ:SSQ + 1], sq_scr[:], axis=AXX)
                v.drain()
                mp = MEAN0 + ti % 2
                v.tensor_scalar_mul(sw[:, mp:mp + 1], sw[:, MEAN:MEAN + 1], 1.0 / M)
                v.drain()
                v.tensor_mul(sw[:, T1:T1 + 1], sw[:, mp:mp + 1], sw[:, mp:mp + 1])
                v.drain()
                return v.scalar_tensor_tensor(sw[:, VAR:VAR + 1],
                                              sw[:, SSQ:SSQ + 1], 1.0 / M,
                                              sw[:, T1:T1 + 1], MULT, SUB)
            K.op("v", [(sWo, sWo.n), (sXR[ti % 2], 16 * (ti // 2 + 1)),
                       (sSt, sSt.n),
                       (sZS[ti % 2], 16 * max(0, (ti - 2) // 2 + 1))], va, sZv)
            if "zpre" in dbg:
                K.dma("q", [(sZv, sZv.n)], dbg["zpre"][ti], zt[ti % 2][:], sY)
            vb_extra = [(sY, sY.n)] if "zpre" in dbg else []
            K.op("s", [(sZv, sZv.n)],
                 lambda s, ti=ti: s.activation(sw[:, STD0 + ti % 2:STD0 + ti % 2 + 1],
                                               sw[:, VAR:VAR + 1], ACT.Sqrt,
                                               bias=eps_col[:]), sSt)

            def vb(v, ti=ti):
                zb = zt[ti % 2]
                v.reciprocal(sw[:, INV:INV + 1], sw[:, STD0 + ti % 2:STD0 + ti % 2 + 1])
                v.drain()
                v.tensor_scalar(zb[:], zb[:], sw[:, MEAN0 + ti % 2:MEAN0 + ti % 2 + 1],
                                sw[:, INV:INV + 1], SUB, MULT)
                v.reduce_max(sw[:, T1:T1 + 1], zb[:], axis=AXX,
                             apply_absolute_value=True)
                v.drain()
                return v.tensor_max(sw[:, ZMAX:ZMAX + 1], sw[:, ZMAX:ZMAX + 1],
                                    sw[:, T1:T1 + 1])
            K.op("v", [(sSt, sSt.n)] + vb_extra, vb, sZv)
            K.dma("q", [(sZv, sZv.n)], z_dram[ti], zt[ti % 2][:], sZS[ti % 2])
            if "z" in dbg:
                K.dma("q", [], dbg["z"][ti], zt[ti % 2][:], sY)

        if "swd" in dbg:
            K.dma("q", [(sZv, sZv.n)], dbg["swd"], sw[:], sY)
        K.op("t", [(sZv, sZv.n)],
             lambda t: t.transpose(misc_ps[0:1, 0:128], sw[:, ZMAX:ZMAX + 1],
                                   identF[:]), sWt)
        if "mps" in dbg:
            def mpscp(v):
                return v.tensor_copy(scr[:, 0:512], misc_ps[:])
            K.op("v", [(sWt, sWt.n)], mpscp, sGl)
            K.dma("q", [(sGl, sGl.n)], dbg["mps"], scr[:, 0:512], sY)
        K.op("v", [(sWt, sWt.n)],
             lambda v: v.reduce_max(rw2[0:1, 0:1], misc_ps[0:1, 0:128], axis=AXX), sGl)
        K.dma("g", [(sGl, sGl.n)], cc_in[0:1, 0:1], rw2[0:1, 0:1], sGB)

        K.op("g", [(sGB, sGB.n)],
             lambda g: g.collective_compute("AllReduce", AMAX,
                                            replica_groups=[list(range(NC))],
                                            ins=[cc_in], outs=[cc_out]), sCc)
        K.dma("g", [(sCc, sCc.n)], rw2[0:1, 4:5], cc_out[0:1, 0:1], sGB)
        n_gm1 = sGB.n
        K.op("s", [(sC, n_const)],
             lambda s: s.activation(rw2[0:1, 5:6], clip_sb[0:1, 0:1], ACT.Sigmoid),
             sSt)

        def rx_chain(v):
            v.tensor_mul(rw2[0:1, 6:7], rw2[0:1, 4:5], rw2[0:1, 5:6])
            v.drain()
            v.reciprocal(rw2[0:1, 7:8], rw2[0:1, 6:7])
            v.drain()
            v.tensor_scalar_mul(rw2[0:1, 7:8], rw2[0:1, 7:8], 127.0)
            v.drain()
            return v.tensor_scalar_mul(rw2[0:1, 8:9], rw2[0:1, 6:7],
                                       1.0 / (127.0 * 127.0))
        K.op("v", [(sGB, n_gm1), (sSt, sSt.n)], rx_chain, sGl)

        def bc1(t):
            t.matmul(misc_ps[:, 0:1], onesF[:], rw2[0:1, 7:8], start=True, stop=True)
            return t.matmul(misc_ps[:, 1:2], onesF[:], rw2[0:1, 8:9],
                            start=True, stop=True)
        K.op("t", [(sGl, sGl.n)], bc1, sWt)

        def bc1c(v):
            v.tensor_copy(sw[:, RXC:RXC + 1], misc_ps[:, 0:1])
            return v.tensor_copy(sw[:, SXD:SXD + 1], misc_ps[:, 1:2])
        K.op("v", [(sWt, sWt.n)], bc1c, sGl)
        n_gl1 = sGl.n
        if "gmax" in dbg:
            K.dma("q", [(sGl, n_gl1)], dbg["gmax"][0:1, 0:4], rw2[0:1, 4:8], sY)

        wop.close()

        xq_fm = postx.enter_context(nc.sbuf_tensor("xq_fm", [128, FCH * 2048], BF16))
        qscope = postx.enter_context(contextlib.ExitStack())
        tq_ps = [qscope.enter_context(nc.psum_tensor(f"tq_ps{i}", [128, 1024], BF16))
                 for i in range(2)]
        for ti in range(NT):
            w = [(sXq, sXq.n - 1)]
            if ti < 2:
                w += [(sZS[0], sZS[0].n), (sZS[1], sZS[1].n)]
            K.dma("q", w, zt[ti % 2][:], z_dram[ti], sZR[ti % 2])

            def qv(v, ti=ti):
                v.tensor_scalar_mul(scr[:], zt[ti % 2][:], sw[:, RXC:RXC + 1])
                v.tensor_copy(scri[:], scr[:])
                v.tensor_scalar(scri[:], scri[:], -128, 127, AMAX, AMIN)
                return v.tensor_copy(xqt[:], scri[:])
            K.op("v", [(sZR[ti % 2], 16 * (ti // 2 + 1)), (sGl, n_gl1), (sTq, sTq.n - 1)], qv, sXq)
            for fc in range(FCH):
                K.op("t", [(sXq, sXq.n), (sDe, sDe.n - 1)],
                     lambda t, ti=ti, fc=fc: t.transpose(
                         tq_ps[fc % 2][:, 0:128], xqt[:, fc * 128:(fc + 1) * 128],
                         identB[:]), sTq)
                K.op("v", [(sTq, sTq.n)],
                     lambda v, ti=ti, fc=fc: v.tensor_copy(
                         xq_fm[:, fc * 2048 + ti * 128:fc * 2048 + (ti + 1) * 128],
                         tq_ps[fc % 2][:, 0:128]), sDe)
        n_xq = sDe.n
        if "xq" in dbg:
            K.dma("g", [(sDe, n_xq)], dbg["xq"], xq_fm[:], sY)
        qscope.close()

        ffa = postx.enter_context(contextlib.ExitStack())
        EG = ffa.enter_context
        gu_ps = [EG(nc.psum_tensor(f"gu_ps{i}", [128, 1024], F32)) for i in range(2)]

        def compute(ic):
            def sgcmul(v, ic=ic):
                v.tensor_mul(sgc_all[:, ic:ic + 1],
                             wmax_all[:, ic:ic + 1], sw[:, SXD:SXD + 1])
                return v.tensor_mul(suc_all[:, ic:ic + 1],
                                    wmax_all[:, ICH + ic:ICH + ic + 1],
                                    sw[:, SXD:SXD + 1])
            K.op("v", [(sGl, n_gl1), (sWq, 2 * (ic + 1))], sgcmul, sSg)
            for tg4 in range(4):
                def gu(t, ic=ic, tg4=tg4):
                    for fc in range(FCH):
                        t.matmul(gu_ps[tg4 % 2][:, 0:512],
                                 wgT[ic % 3][:, fc * 128:(fc + 1) * 128],
                                 xq_fm[:, fc * 2048 + tg4 * 512:fc * 2048 + (tg4 + 1) * 512],
                                 start=(fc == 0), stop=(fc == FCH - 1))
                    for fc in range(FCH):
                        mm = t.matmul(gu_ps[tg4 % 2][:, 512:1024],
                                      wuT[ic % 3][:, fc * 128:(fc + 1) * 128],
                                      xq_fm[:, fc * 2048 + tg4 * 512:fc * 2048 + (tg4 + 1) * 512],
                                      start=(fc == 0), stop=(fc == FCH - 1))
                    return mm
                K.op("t", [(sWe, sWe.n - 2 * (min(ic + 2, ICH) - 1 - ic)),
                           (sDe, n_xq), (sH, sH.n - 1)], gu, sG)
                K.op("s", [(sG, sG.n), (sH, sH.n), (sSg, sSg.n)],
                     lambda s, ic=ic, tg4=tg4: s.activation(
                         silu_sb[:], gu_ps[tg4 % 2][:, 0:512], ACT.Silu,
                         scale=sgc_all[:, ic:ic + 1]), sSi)

                K.op("s", [(sH, sH.n)],
                     lambda s, ic=ic, tg4=tg4: s.activation(
                         ups_sb[:], gu_ps[tg4 % 2][:, 512:1024], ACT.Copy,
                         scale=suc_all[:, ic:ic + 1]), sSi)

                def hv(v, ic=ic, tg4=tg4):
                    col = ic * 2048 + tg4 * 512
                    mm = v.tensor_mul(h_sb[:, col:col + 512], silu_sb[:], ups_sb[:])
                    if tg4 == 3:
                        v.reduce_max(sw[:, T1:T1 + 1],
                                     h_sb[:, ic * 2048:(ic + 1) * 2048], axis=AXX,
                                     apply_absolute_value=True)
                        v.drain()
                        mm = v.tensor_max(sw[:, HMAX:HMAX + 1],
                                          sw[:, HMAX:HMAX + 1], sw[:, T1:T1 + 1])
                    return mm
                K.op("v", [(sSi, sSi.n)], hv, sH)

        for ic in range(ICH):
            if ic + 2 < ICH:
                prepare_vs(ic + 2)
            compute(ic)
            if ic + 2 < ICH:
                prepare_tr(ic + 2)
            if ic + 4 < ICH:
                prepare_load(ic + 4)

        if "sgc" in dbg:
            def sgdump(q):
                return q.dma_start(out=dbg["sgc"][:, 0:2 * ICH], in_=wmax_all[:])
            K.op("q", [(sH, sH.n)], sgdump, sY, 16)
            K.dma("q", [], dbg["sgc"][:, 2 * ICH:3 * ICH], sgc_all[:], sY)
            K.dma("q", [], dbg["sgc"][:, 3 * ICH:4 * ICH], suc_all[:], sY)

        K.op("t", [(sH, sH.n)],
             lambda t: t.transpose(misc_ps[0:1, 0:128], sw[:, HMAX:HMAX + 1],
                                   identF[:]), sWt)
        K.op("v", [(sWt, sWt.n)],
             lambda v: v.reduce_max(rw2[0:1, 9:10], misc_ps[0:1, 0:128], axis=AXX),
             sGl)
        K.dma("g", [(sGl, sGl.n)], cc2_in[0:1, 0:1], rw2[0:1, 9:10], sGB)
        K.op("g", [(sGB, sGB.n)],
             lambda g: g.collective_compute("AllReduce", AMAX,
                                            replica_groups=[list(range(NC))],
                                            ins=[cc2_in], outs=[cc2_out]), sCc)
        K.dma("g", [(sCc, sCc.n)], rw2[0:1, 10:11], cc2_out[0:1, 0:1], sGB)
        n_gm2 = sGB.n
        K.op("s", [(sC, n_const)],
             lambda s: s.activation(rw2[0:1, 11:12], clip_sb[0:1, 1:2], ACT.Sigmoid),
             sSt)

        def rh_chain(v):
            v.tensor_mul(rw2[0:1, 12:13], rw2[0:1, 10:11], rw2[0:1, 11:12])
            v.drain()
            v.reciprocal(rw2[0:1, 13:14], rw2[0:1, 12:13])
            v.drain()
            v.tensor_scalar_mul(rw2[0:1, 13:14], rw2[0:1, 13:14], 127.0)
            v.drain()
            return v.tensor_scalar_mul(rw2[0:1, 14:15], rw2[0:1, 12:13],
                                       1.0 / (127.0 * 127.0))
        K.op("v", [(sGB, n_gm2), (sSt, sSt.n)], rh_chain, sGl)

        def bc2(t):
            t.matmul(misc_ps[:, 2:3], onesF[:], rw2[0:1, 13:14], start=True, stop=True)
            return t.matmul(misc_ps[:, 3:4], onesF[:], rw2[0:1, 14:15],
                            start=True, stop=True)
        K.op("t", [(sGl, sGl.n)], bc2, sWt)

        def bc2c(v):
            v.tensor_copy(sw[:, RHC:RHC + 1], misc_ps[:, 2:3])
            return v.tensor_copy(sw[:, SHD:SHD + 1], misc_ps[:, 3:4])
        K.op("v", [(sWt, sWt.n)], bc2c, sGl)
        n_gl2 = sGl.n
        if "gmax" in dbg:
            K.dma("q", [(sGl, n_gl2)], dbg["gmax"][0:1, 4:8], rw2[0:1, 10:14], sY)


        ffa.close()
        postx.close()

        ffd = post.enter_context(contextlib.ExitStack())
        ED = ffd.enter_context
        wdT = ED(nc.sbuf_tensor("wdT", [128, ICH * M], BF16))
        wdqD = [ED(nc.sbuf_tensor(f"wdqD{i}", [128, IDIM], BF16)) for i in range(2)]
        row_sb = ED(nc.sbuf_tensor("row_sb", [1, M], F32))
        bcast_sb = ED(nc.sbuf_tensor("bcast_sb", [128, M], BF16))
        va2scr = ED(nc.sbuf_tensor("va2scr", [128, M], F32))
        twd_ps = [ED(nc.psum_tensor(f"twd_ps{i}", [128, 128], BF16)) for i in range(2)]
        dn_ps = [ED(nc.psum_tensor(f"dn_ps{i}", [128, M], F32)) for i in range(2)]
        wdTi3 = wdT[:].rearrange("p (i m) -> p i m", i=ICH)
        n_dq = sDQ.n

        for hc in range(FCH):
            w = [(sDQ, n_dq), (sWt, sWt.n - 22)]
            if hc == 0:
                w.append((sH, sH.n))
            K.dma("q", w, wdqD[hc % 2][:], wdq_dram[hc], sDW[hc % 2])
            for ic in range(ICH):
                K.op("t", [(sDW[hc % 2], 16 * (hc // 2 + 1)), (sWe, sWe.n - 1)],
                     lambda t, hc=hc, ic=ic: t.transpose(
                         twd_ps[ic % 2][:, 0:128],
                         wdqD[hc % 2][:, ic * 128:(ic + 1) * 128], identB[:]), sWt)
                K.op("v", [(sWt, sWt.n)],
                     lambda v, hc=hc, ic=ic: v.tensor_copy(
                         wdT[:, ic * M + hc * 128:ic * M + (hc + 1) * 128],
                         twd_ps[ic % 2][:, 0:128]), sWe)

        def sdctr(t, h0):
            for hc in range(h0, h0 + 4):
                mm = t.transpose(misc_ps[0:1, (hc - h0) * 128:(hc - h0 + 1) * 128],
                                 wmaxd[:, hc:hc + 1], identF[:])
            return mm
        K.op("t", [(sWR, sWR.n), (sGl, n_gl2)],
             lambda t: sdctr(t, 0), sWt)
        K.op("v", [(sWt, sWt.n)],
             lambda v: v.tensor_copy(row_sb[0:1, 0:512], misc_ps[0:1, 0:512]), sGl)
        K.op("t", [(sGl, sGl.n)], lambda t: sdctr(t, 4), sWt)

        def sdcrow(v):
            v.tensor_copy(row_sb[0:1, 512:M], misc_ps[0:1, 0:512])
            v.drain()
            return v.tensor_scalar_mul(row_sb[0:1, :], row_sb[0:1, :],
                                       sw[0:1, SHD:SHD + 1])
        K.op("v", [(sWt, sWt.n)], sdcrow, sGl)

        def sdcbc(t):
            t.matmul(dn_ps[0][:, 0:512], onesF[:], row_sb[0:1, 0:512],
                     start=True, stop=True)
            return t.matmul(dn_ps[0][:, 512:M], onesF[:], row_sb[0:1, 512:M],
                            start=True, stop=True)
        K.op("t", [(sGl, sGl.n)], sdcbc, sWt)
        K.op("v", [(sWt, sWt.n)],
             lambda v: v.tensor_copy(bcast_sb[:], dn_ps[0][:]), sGl)
        n_bcast = sGl.n

        for ic in range(ICH):
            K.op("v", [(sGl, n_bcast)] if ic == 0 else [],
                 lambda v, ic=ic: v.tensor_mul(wdTi3[:, ic, :], wdTi3[:, ic, :],
                                               bcast_sb[:]), sWe)
        n_wdT = sWe.n
        n_wdq = sWq.n
        if "row" in dbg:
            K.dma("q", [(sGl, n_bcast)], dbg["row"], row_sb[:], sY)
        if "wmaxd" in dbg:
            K.dma("q", [(sWq, n_wdq)], dbg["wmaxd"], wmaxd[:], sY)
        if "wdt" in dbg:
            K.dma("g", [(sWe, n_wdT)], dbg["wdt"], wdT[:], sY)
        h3 = h_sb[:].rearrange("p (i t) -> p i t", i=ICH)
        HB = ((0, 8), (8, 16), (16, ICH))

        def emit_hr(ti):
            for ci, (i0, i1) in enumerate(HB):
                nel = (i1 - i0) * 128

                def hs_ap(ti=ti, i0=i0, i1=i1):
                    return h3[:, i0:i1, ti * 128:(ti + 1) * 128]

                def hmul(ss, ti=ti, i0=i0, i1=i1, nel=nel):
                    return ss.mul(scr[:, 0:nel].rearrange(
                        "p (a b) -> p a b", a=i1 - i0), hs_ap(ti, i0, i1),
                        sw[:, RHC:RHC + 1])
                w = [(sGl, n_gl2), (sWq, n_wdq)]
                if sHr.n > 0:
                    w.append((sHr, sHr.n - 0))
                K.op("s", w, hmul, sHs)

                def hcvt(v, nel=nel):
                    v.tensor_copy(scri[:, 0:nel], scr[:, 0:nel])
                    return v.tensor_scalar(scri[:, 0:nel], scri[:, 0:nel],
                                           -128, 127, AMAX, AMIN)
                K.op("v", [(sHs, sHs.n)], hcvt, sHr)

                def hback(ss, ti=ti, i0=i0, i1=i1, nel=nel):
                    return ss.copy(hs_ap(ti, i0, i1), scri[:, 0:nel].rearrange(
                        "p (a b) -> p a b", a=i1 - i0))
                K.op("s", [(sHr, sHr.n)], hback, sHs)

        emit_hr(0)
        emit_hr(1)

        for ti in range(NT):
            b_, tc = ti // 8, ti % 8
            w = [(sXq, sXq.n), (sZv, sZv.n - 4),
                 (sYP[ti % 2], 16 * max(0, (ti - 2) // 2 + 1))]
            if ti < 2:
                w += [(sZR[0], sZR[0].n), (sZR[1], sZR[1].n)]
            K.dma("q", w, zt[ti % 2][:], z_dram[ti], sZD[ti % 2])

            def dmm(t, ti=ti):
                for ic in range(ICH):
                    hcol = ic * 2048 + ti * 128
                    t.matmul(dn_ps[ti % 2][:, 0:512], h_sb[:, hcol:hcol + 128],
                             wdT[:, ic * M:ic * M + 512],
                             start=(ic == 0), stop=(ic == ICH - 1))
                    mm = t.matmul(dn_ps[ti % 2][:, 512:M], h_sb[:, hcol:hcol + 128],
                                  wdT[:, ic * M + 512:(ic + 1) * M],
                                  start=(ic == 0), stop=(ic == ICH - 1))
                return mm
            K.op("t", [(sWe, n_wdT), (sHs, 6 * (ti + 1)), (sZv, sZv.n - 3)],
                 dmm, sD)
            if ti + 2 < NT:
                emit_hr(ti + 2)

            def va2(v, ti=ti):
                zb = zt[ti % 2]
                v.scalar_tensor_tensor(zb[:, 0:512], dn_ps[ti % 2][:, 0:512],
                                       1.0, zb[:, 0:512], MULT, ADD)
                v.scalar_tensor_tensor(zb[:, 512:M], dn_ps[ti % 2][:, 512:M],
                                       1.0, zb[:, 512:M], MULT, ADD)
                v.tensor_mul(va2scr[:], zb[:], zb[:])
                v.reduce_sum(sw[:, MEAN:MEAN + 1], zb[:], axis=AXX)
                v.reduce_sum(sw[:, SSQ:SSQ + 1], va2scr[:], axis=AXX)
                v.drain()
                mp = MEAN0 + ti % 2
                v.tensor_scalar_mul(sw[:, mp:mp + 1], sw[:, MEAN:MEAN + 1], 1.0 / M)
                v.drain()
                v.tensor_mul(sw[:, T1:T1 + 1], sw[:, mp:mp + 1], sw[:, mp:mp + 1])
                v.drain()
                return v.scalar_tensor_tensor(sw[:, VAR:VAR + 1],
                                              sw[:, SSQ:SSQ + 1], 1.0 / M,
                                              sw[:, T1:T1 + 1], MULT, SUB)
            K.op("v", [(sD, sD.n), (sZD[ti % 2], 16 * (ti // 2 + 1)), (sSt, sSt.n), (sWq, n_wdq)],
                 va2, sZv)
            K.op("s", [(sZv, sZv.n)],
                 lambda s, ti=ti: s.activation(sw[:, STD0 + ti % 2:STD0 + ti % 2 + 1],
                                               sw[:, VAR:VAR + 1], ACT.Sqrt,
                                               bias=eps_col[:]), sSt)

            def vb2(v, ti=ti):
                zb = zt[ti % 2]
                v.reciprocal(sw[:, INV:INV + 1], sw[:, STD0 + ti % 2:STD0 + ti % 2 + 1])
                v.drain()
                return v.tensor_scalar(zb[:], zb[:],
                                       sw[:, MEAN0 + ti % 2:MEAN0 + ti % 2 + 1],
                                       sw[:, INV:INV + 1], SUB, MULT)
            K.op("v", [(sSt, sSt.n)], vb2, sZv)
            K.dma("q", [(sZv, sZv.n)], y_out[b_, tc * 128:(tc + 1) * 128, :],
                  zt[ti % 2][:], sYP[ti % 2])

        if "hh" in dbg:
            K.dma("g", [(sHs, sHs.n)], dbg["hh"], h_sb[:], sY)
        n_y0, n_y1, n_yd = sYP[0].n, sYP[1].n, sY.n

        def _fence(eng):
            eng.wait_ge(sYP[0].h, n_y0)
            eng.wait_ge(sYP[1].h, n_y1)
            if n_yd:
                eng.wait_ge(sY.h, n_yd)
        K.ops["q"].append(_fence)

        @block.tensor
        def _(t):
            for fn in K.ops["t"]:
                fn(t)

        @block.scalar
        def _(s):
            for fn in K.ops["s"]:
                fn(s)

        @block.vector
        def _(v):
            for fn in K.ops["v"]:
                fn(v)

        @block.sync
        def _(q):
            for fn in K.ops["q"]:
                fn(q)

        @block.gpsimd
        def _(g):
            for fn in K.ops["g"]:
                fn(g)

    return nc


_NC_CACHE = {}


def _get_nc(debug=()):
    key = tuple(sorted(debug))
    if key not in _NC_CACHE:
        _NC_CACHE[key] = build(debug=key)
    return _NC_CACHE[key]


def kernel(L0_attn_x, L0_attn_wq, L0_attn_wk, L0_attn_wv, L0_attn_wo,
           L0_ffn_clip_x, L0_ffn_clip_down, L0_ffn_w_gate, L0_ffn_w_up,
           L0_ffn_w_down, _debug=()):
    x = np.ascontiguousarray(np.asarray(L0_attn_x, dtype=np.float32))
    wq = np.ascontiguousarray(np.asarray(L0_attn_wq, dtype=np.float32))
    wk = np.ascontiguousarray(np.asarray(L0_attn_wk, dtype=np.float32))
    wv = np.ascontiguousarray(np.asarray(L0_attn_wv, dtype=np.float32))
    wo = np.ascontiguousarray(np.asarray(L0_attn_wo, dtype=np.float32))
    wg = np.ascontiguousarray(np.asarray(L0_ffn_w_gate, dtype=np.float32))
    wu = np.ascontiguousarray(np.asarray(L0_ffn_w_up, dtype=np.float32))
    wd = np.ascontiguousarray(np.asarray(L0_ffn_w_down, dtype=np.float32))
    clip2 = np.stack(
        [np.asarray(L0_ffn_clip_x, dtype=np.float32).reshape(()),
         np.asarray(L0_ffn_clip_down, dtype=np.float32).reshape(())]
    ).reshape(1, 2)

    nc_prog = _get_nc(_debug)
    in_maps = []
    for c in range(NC):
        b = c // 4
        h0 = 2 * (c % 4)
        sl = slice(c * SL, (c + 1) * SL)
        in_maps.append({
            "xb": x[b],
            "xres": np.ascontiguousarray(x[:, sl, :]),
            "wq3": np.ascontiguousarray(wq[:, h0:h0 + 2, :]),
            "wk3": np.ascontiguousarray(wk[:, h0:h0 + 2, :]),
            "wv3": np.ascontiguousarray(wv[:, h0:h0 + 2, :]),
            "wo4": wo,
            "wgd": wg,
            "wud": wu,
            "wdd": wd,
            "clips": clip2,
        })
    res = run_bass_kernel_spmd(nc_prog, in_maps, core_ids=list(range(NC)))
    y = np.empty((B, T, M), dtype=np.float32)
    for c in range(NC):
        sl = slice(c * SL, (c + 1) * SL)
        y[:, sl, :] = res.results[c]["y"]
    if _debug:
        return y, res
    return y


# version 19
# speedup vs baseline: 1.1193x; 1.0504x over previous
import contextlib
import numpy as np
import ml_dtypes
import concourse.bass as bass
import concourse.mybir as mybir
from concourse.bass_utils import run_bass_kernel_spmd
from concourse.alu_op_type import AluOpType
from concourse.aot_env import allow_ici_all_to_all

F32 = mybir.dt.float32
BF16 = mybir.dt.bfloat16
I32 = mybir.dt.int32
ACT = mybir.ActivationFunctionType
AXX = mybir.AxisListType.X
NBF = ml_dtypes.bfloat16
MULT = AluOpType.mult
ADD = AluOpType.add
SUB = AluOpType.subtract
AMAX = AluOpType.max
AMIN = AluOpType.min

B, T, M, H, D = 2, 8192, 1024, 8, 128
NC = 8
SL = T // NC
NT = 16
IDIM = 2816
ICH = 22
FCH = 8
QG = 1024
NG = T // QG
RSC = float(D) ** -0.5


class Sem:
    def __init__(self, es, nc, name):
        self.h = es.enter_context(nc.semaphore(name))
        self.n = 0


class Sched:
    ENGS = ("t", "s", "v", "q", "g")

    def __init__(self):
        self.ops = {e: [] for e in self.ENGS}

    def op(self, ek, waits, emitfn, inc=None, inck=1):
        wl = [(s.h, v) for (s, v) in waits if v > 0]
        if inc is not None:
            inc.n += inck

        def thunk(eng, wl=wl, emitfn=emitfn, inc=inc, inck=inck):
            for hh, vv in wl:
                eng.wait_ge(hh, vv)
            instr = emitfn(eng)
            if inc is not None:
                instr.then_inc(inc.h, inck)
        self.ops[ek].append(thunk)

    def dma(self, ek, waits, out, in_, inc):
        self.op(ek, waits, lambda e, o=out, i=in_: e.dma_start(out=o, in_=i),
                inc, 16)


def _tables():
    w32 = np.arange(32)
    ang = 10000.0 ** (-w32 / 32.0)
    rad = np.arange(T)[:, None] * ang[None, :]
    cosT = np.concatenate([np.cos(rad).T, np.cos(rad).T], 0).astype(NBF)
    sinT = np.concatenate([-np.sin(rad).T, np.sin(rad).T], 0).astype(NBF)
    kk = np.arange(128)[:, None]
    jj = np.arange(2 * QG)[None, :]
    band = (jj >= kk + QG).astype(NBF)
    return cosT, sinT, band


def build(debug=()):
    nc = bass.Bass("TRN2", target_bir_lowering=False, debug=False)

    xb = nc.dram_tensor("xb", [T, M], F32, kind="ExternalInput").ap()
    xres = nc.dram_tensor("xres", [B, SL, M], F32, kind="ExternalInput").ap()
    wq3 = nc.dram_tensor("wq3", [M, 2, D], F32, kind="ExternalInput").ap()
    wk3 = nc.dram_tensor("wk3", [M, 2, D], F32, kind="ExternalInput").ap()
    wv3 = nc.dram_tensor("wv3", [M, 2, D], F32, kind="ExternalInput").ap()
    wo4 = nc.dram_tensor("wo4", [H, D, M], F32, kind="ExternalInput").ap()
    wgd = nc.dram_tensor("wgd", [IDIM, M], F32, kind="ExternalInput").ap()
    wud = nc.dram_tensor("wud", [IDIM, M], F32, kind="ExternalInput").ap()
    wdd = nc.dram_tensor("wdd", [M, IDIM], F32, kind="ExternalInput").ap()
    clips = nc.dram_tensor("clips", [1, 2], F32, kind="ExternalInput").ap()
    y_out = nc.dram_tensor("y", [B, SL, M], F32, kind="ExternalOutput").ap()

    a2a_in = [nc.dram_tensor(f"a2a_in{hl}", [NC, 128, QG], BF16, kind="Internal").ap()
              for hl in range(2)]
    a2a_out = [nc.dram_tensor(f"a2a_out{hl}", [NC, 128, QG], BF16, kind="Internal").ap()
               for hl in range(2)]
    cc_in = nc.dram_tensor("cc_in", [1, 1], F32, kind="Internal").ap()
    cc_out = nc.dram_tensor("cc_out", [1, 1], F32, kind="Internal").ap()
    cc2_in = nc.dram_tensor("cc2_in", [1, 1], F32, kind="Internal").ap()
    cc2_out = nc.dram_tensor("cc2_out", [1, 1], F32, kind="Internal").ap()
    z_dram = nc.dram_tensor("z_dram", [NT, 128, M], F32, kind="Internal").ap()
    wdq_dram = nc.dram_tensor("wdq_dram", [FCH, 128, IDIM], BF16, kind="Internal").ap()

    cosT_np, sinT_np, band_np = _tables()
    c_cos = nc.inline_tensor(cosT_np, "c_cos").ap()
    c_sin = nc.inline_tensor(sinT_np, "c_sin").ap()
    c_band = nc.inline_tensor(band_np, "c_band").ap()
    c_identB = nc.inline_tensor(np.eye(128, dtype=NBF), "c_identB").ap()
    c_identF = nc.inline_tensor(np.eye(128, dtype=np.float32), "c_identF").ap()
    c_onesB = nc.inline_tensor(np.ones((128, 1), dtype=NBF), "c_onesB").ap()
    c_onesF = nc.inline_tensor(np.ones((1, 128), dtype=np.float32), "c_onesF").ap()
    c_eps = nc.inline_tensor(np.full((128, 1), 1e-5, dtype=np.float32), "c_eps").ap()

    dbg = {}
    for name, shape in (("z", [NT, 128, M]), ("zpre", [NT, 128, M]),
                        ("sgc", [128, 4 * ICH]),
                        ("swd", [128, 32]), ("mps", [128, 512]),
                        ("row", [1, 1024]), ("wmaxd", [128, 8]),
                        ("wdt", [128, ICH * 1024]),
                        ("hh", [128, ICH * 2048]),
                        ("gmax", [1, 8]), ("q0", [128, T]), ("o0", [128, T]),
                        ("q1", [128, T]), ("k1", [128, T]), ("o1", [128, T]),
                        ("oall", [128, 16 * QG]), ("xq", [128, FCH * 2048])):
        if name in debug:
            dbg[name] = nc.dram_tensor("dbg_" + name, shape, F32,
                                       kind="ExternalOutput").ap()

    K = Sched()
    es = contextlib.ExitStack()
    with es:
        E = es.enter_context
        identB = E(nc.sbuf_tensor("identB", [128, 128], BF16))
        identF = E(nc.sbuf_tensor("identF", [128, 128], F32))
        onesB = E(nc.sbuf_tensor("onesB", [128, 1], BF16))
        onesF = E(nc.sbuf_tensor("onesF", [1, 128], F32))

        sw = E(nc.sbuf_tensor("sw", [128, 32], F32))
        rw2 = E(nc.sbuf_tensor("rw2", [1, 64], F32))
        clip_sb = E(nc.sbuf_tensor("clip_sb", [1, 2], F32))
        eps_col = E(nc.sbuf_tensor("eps_col", [128, 1], F32))
        wmaxd = E(nc.sbuf_tensor("wmaxd", [128, 8], F32))

        MEAN, SSQ, VAR, T1, INV = 0, 1, 2, 3, 4
        ZMAX, RXC, SXD, WMX, SGC, SUC = 5, 6, 7, 8, 9, 10
        HMAX, RHC, SHD, SDC = 11, 12, 13, 14
        MEAN0, STD0 = 16, 18

        S = {}
        for name in ("sC sX sW sT sE sQ sQE sVE sSw sR sM sEx sMk sPv sRc "
                     "sNm sA sCc sOa sWo sZv sSt sZs sXq sTq sDe sG sSi sH "
                     "sWq sWt sWe sD sY sHr sGl sXC sHs sWS sSg "
                     "sWX0 sWX1 sWX2 sXR0 sXR1 sZS0 sZS1 sZR0 sZR1 sZD0 sZD1 "
                     "sLG sLU sDF0 sDF1 sWO sGB sA0 sA1 sTB sYP0 sYP1 sDQ sDW0 sDW1 sWR").split():
            S[name] = Sem(es, nc, name)
        sC, sX, sW, sT, sE = S["sC"], S["sX"], S["sW"], S["sT"], S["sE"]
        sQ, sQE, sVE, sSw, sR = S["sQ"], S["sQE"], S["sVE"], S["sSw"], S["sR"]
        sM, sEx, sMk, sPv, sRc = S["sM"], S["sEx"], S["sMk"], S["sPv"], S["sRc"]
        sNm, sA, sCc, sOa, sWo = S["sNm"], S["sA"], S["sCc"], S["sOa"], S["sWo"]
        sZv, sSt, sZs, sXq, sTq = S["sZv"], S["sSt"], S["sZs"], S["sXq"], S["sTq"]
        sDe, sG, sSi, sH, sWq = S["sDe"], S["sG"], S["sSi"], S["sH"], S["sWq"]
        sWt, sWe, sD, sY, sHr, sGl = (S["sWt"], S["sWe"], S["sD"], S["sY"],
                                      S["sHr"], S["sGl"])
        sXC = S["sXC"]
        sHs = S["sHs"]
        sWS = S["sWS"]
        sSg = S["sSg"]
        sWX = [S["sWX0"], S["sWX1"], S["sWX2"]]
        sXR = [S["sXR0"], S["sXR1"]]
        sZS = [S["sZS0"], S["sZS1"]]
        sZR = [S["sZR0"], S["sZR1"]]
        sZD = [S["sZD0"], S["sZD1"]]
        sLG, sLU = S["sLG"], S["sLU"]
        sDF = [S["sDF0"], S["sDF1"]]
        sWO, sGB, sTB = S["sWO"], S["sGB"], S["sTB"]
        sAp = [S["sA0"], S["sA1"]]
        sYP = [S["sYP0"], S["sYP1"]]
        sDQ = S["sDQ"]
        sDW = [S["sDW0"], S["sDW1"]]
        sWR = S["sWR"]

        block = E(nc.Block())

        for dst, src in ((identB[:], c_identB), (identF[:], c_identF),
                         (onesB[:], c_onesB), (onesF[:], c_onesF),
                         (clip_sb[:], clips),
                         (eps_col[:], c_eps)):
            K.dma("q", [], dst, src, sC)

        att = es.enter_context(contextlib.ExitStack())
        EA = att.enter_context
        band = EA(nc.sbuf_tensor("band", [128, 2 * QG], BF16))
        rw = EA(nc.sbuf_tensor("rw", [1, 2 * QG], F32))
        cosT = EA(nc.sbuf_tensor("cosT", [64, T // 4], BF16))
        sinT = EA(nc.sbuf_tensor("sinT", [64, T // 4], BF16))
        wq_sb = EA(nc.sbuf_tensor("wq_sb", [128, 2 * FCH * D], BF16))
        wk_sb = EA(nc.sbuf_tensor("wk_sb", [128, 2 * FCH * D], BF16))
        wv_mv = EA(nc.sbuf_tensor("wv_mv", [128, FCH * 2 * D], BF16))
        QF = [EA(nc.sbuf_tensor(f"QF{h}", [128, T], BF16)) for h in range(2)]
        KF = [EA(nc.sbuf_tensor(f"KF{h}", [128, T], BF16)) for h in range(2)]
        VT = [EA(nc.sbuf_tensor(f"VT{h}", [128, T], BF16)) for h in range(2)]
        swp = EA(nc.sbuf_tensor("swp", [64, T // 4], BF16))
        tmp2 = EA(nc.sbuf_tensor("tmp2", [64, T // 4], BF16))
        pt = [EA(nc.sbuf_tensor(f"pt{i}", [128, QG], BF16)) for i in range(3)]
        obuf = [EA(nc.sbuf_tensor(f"obuf{i}", [128, QG], BF16)) for i in range(2)]
        rbc = EA(nc.sbuf_tensor("rbc", [128, QG], BF16))

        K.dma("q", [], band[:], c_band, sC)
        n_const = sC.n

        for h in range(2):
            for fc in range(FCH):
                K.dma("g", [], wq_sb[:, (h * FCH + fc) * D:(h * FCH + fc + 1) * D],
                      wq3[fc * 128:(fc + 1) * 128, h, :], sX)
                K.dma("g", [], wk_sb[:, (h * FCH + fc) * D:(h * FCH + fc + 1) * D],
                      wk3[fc * 128:(fc + 1) * 128, h, :], sX)
        for fc in range(FCH):
            K.dma("g", [], wv_mv[:, fc * 256:(fc + 1) * 256],
                  wv3[fc * 128:(fc + 1) * 128, :, :], sX)
        n_wqkv = sX.n

        xp = att.enter_context(contextlib.ExitStack())
        EX = xp.enter_context
        xtm = [EX(nc.sbuf_tensor(f"xtm{i}", [128, 4096], BF16)) for i in range(2)]
        xf32 = [EX(nc.sbuf_tensor(f"xf32{i}", [128, 1024], F32)) for i in range(2)]
        xfm = [EX(nc.sbuf_tensor(f"xfm{i}", [128, FCH * 512], BF16)) for i in range(2)]
        xt_ps = [EX(nc.psum_tensor(f"xt_ps{i}", [128, 1024], BF16)) for i in range(2)]
        q_ps = [EX(nc.psum_tensor(f"q_ps{h}", [128, 512], F32)) for h in range(2)]
        k_ps = [EX(nc.psum_tensor(f"k_ps{h}", [128, 512], F32)) for h in range(2)]
        v_ps = [EX(nc.psum_tensor(f"v_ps{i}", [128, 512], F32)) for i in range(2)]

        TH = T // 4
        rope_done = {}

        def emit_rope_pair(qt, pr):
            cs = slice(qt * TH, (qt + 1) * TH)
            evac_need = 16 * (qt + 1)
            TTl, h = ((0, 0), (1, 0), (0, 1), (1, 1))[pr]
            TT = (QF, KF)[TTl]
            if pr == 0:
                wtb = [] if sR.n == 0 else [(sR, sR.n - 1)]
                K.dma("g", wtb, cosT[:], c_cos[:, cs], sTB)
                K.dma("g", [], sinT[:], c_sin[:, cs], sTB)
            ntab = sTB.n
            w0 = [(sQE, evac_need)] if sR.n == 0 else [(sR, sR.n - 2),
                                                       (sQE, evac_need)]
            K.dma("g", w0, swp[0:32, :], TT[h][32:64, cs], sSw)
            K.dma("g", [], swp[32:64, :], TT[h][0:32, cs], sSw)
            K.op("v", [(sSw, sSw.n), (sTB, ntab)],
                 lambda v: v.tensor_mul(tmp2[:], swp[:], sinT[:]), sR)
            K.op("v", [],
                 lambda v, TT=TT, h=h, cs=cs: v.tensor_mul(
                     TT[h][0:64, cs], TT[h][0:64, cs], cosT[:]), sR)
            K.op("v", [],
                 lambda v, TT=TT, h=h, cs=cs: v.tensor_add(
                     TT[h][0:64, cs], TT[h][0:64, cs], tmp2[:]), sR)
            rope_done[(TTl, h)] = max(rope_done.get((TTl, h), 0), sR.n)

        for tg in range(16):
            for t4 in range(4):
                ii = tg * 4 + t4
                r0 = tg * 512 + t4 * 128
                K.dma("q", [(sXC, sXC.n - 1)],
                      xf32[ii % 2][:], xb[r0:r0 + 128, :], sWX[ii % 2])
                K.op("s", [(sWX[ii % 2], 16 * (ii // 2 + 1)), (sT, sT.n - 8)],
                     lambda ss, tg=tg, t4=t4, ii=ii: ss.copy(
                         xtm[tg % 2][:, t4 * 1024:(t4 + 1) * 1024], xf32[ii % 2][:]),
                     sXC)
            for fc in range(FCH):
                def tr(t, tg=tg, fc=fc):
                    for t4 in range(4):
                        mm = t.transpose(
                            xt_ps[fc % 2][:, t4 * 128:(t4 + 1) * 128],
                            xtm[tg % 2][:, t4 * 1024 + fc * 128:t4 * 1024 + (fc + 1) * 128],
                            identB[:])
                    return mm
                K.op("t", [(sXC, sXC.n), (sE, sE.n - 1), (sC, n_const)], tr, sT)
                K.op("v", [(sT, sT.n), (sQ, sQ.n - 6)],
                     lambda v, tg=tg, fc=fc: v.tensor_copy(
                         xfm[tg % 2][:, fc * 512:(fc + 1) * 512], xt_ps[fc % 2][:, 0:512]),
                     sE)
            for h in range(2):
                def qk(t, tg=tg, h=h):
                    for fc in range(FCH):
                        t.matmul(q_ps[h][:],
                                 wq_sb[:, (h * FCH + fc) * D:(h * FCH + fc + 1) * D],
                                 xfm[tg % 2][:, fc * 512:(fc + 1) * 512],
                                 start=(fc == 0), stop=(fc == FCH - 1))
                    for fc in range(FCH):
                        mm = t.matmul(k_ps[h][:],
                                      wk_sb[:, (h * FCH + fc) * D:(h * FCH + fc + 1) * D],
                                      xfm[tg % 2][:, fc * 512:(fc + 1) * 512],
                                      start=(fc == 0), stop=(fc == FCH - 1))
                    return mm
                K.op("t", [(sE, sE.n), (sQE, sQE.n - 2), (sX, n_wqkv)], qk, sQ)
                K.op("s", [(sQ, sQ.n)],
                     lambda s, tg=tg, h=h: s.mul(
                         QF[h][:, tg * 512:(tg + 1) * 512], q_ps[h][:], RSC), sQE)
                K.op("s", [],
                     lambda s, tg=tg, h=h: s.mul(
                         KF[h][:, tg * 512:(tg + 1) * 512], k_ps[h][:], RSC), sQE)
            for tch in range(4):
                def vv(t, tg=tg, tch=tch):
                    for fc in range(FCH):
                        mm = t.matmul(
                            v_ps[tch % 2][:, 0:256],
                            xfm[tg % 2][:, fc * 512 + tch * 128:fc * 512 + (tch + 1) * 128],
                            wv_mv[:, fc * 256:(fc + 1) * 256],
                            start=(fc == 0), stop=(fc == FCH - 1))
                    return mm
                K.op("t", [(sVE, sVE.n - 1)], vv, sQ)

                def vev(v, tg=tg, tch=tch):
                    for h in range(2):
                        mm = v.tensor_copy(
                            VT[h][:, (tg * 4 + tch) * 128:(tg * 4 + tch + 1) * 128],
                            v_ps[tch % 2][:, h * 128:(h + 1) * 128])
                    return mm
                K.op("v", [(sQ, sQ.n)], vev, sVE)
            for qt in range(4):
                pr = tg - (4 * qt + 3)
                if 0 <= pr <= 3:
                    emit_rope_pair(qt, pr)

        for pr in range(1, 4):
            emit_rope_pair(3, pr)

        n_qkdone = sQE.n
        n_ve = sVE.n
        xp.close()

        n_rope = sR.n
        rope_unit = {0: rope_done[(1, 0)], 1: n_rope}

        if "q0" in dbg:
            K.dma("g", [(sR, n_rope)], dbg["q0"], QF[0][:], sY)
        if "q1" in dbg:
            K.dma("g", [(sR, n_rope)], dbg["q1"], QF[1][:], sY)
        if "k1" in dbg:
            K.dma("g", [(sR, n_rope)], dbg["k1"], KF[1][:], sY)

        fl = att.enter_context(contextlib.ExitStack())
        EF = fl.enter_context
        st_ps = [EF(nc.psum_tensor(f"st_ps{i}", [128, QG], F32)) for i in range(2)]
        o_ps = EF(nc.psum_tensor("o_ps", [128, QG], F32))
        rsA = EF(nc.psum_tensor("rsA", [128, 512], F32))
        rsB = EF(nc.psum_tensor("rsB", [128, 512], F32))
        wdl3 = [EF(nc.sbuf_tensor(f"wdl3{i}", [128, 1024], F32)) for i in range(3)]
        wscr = EF(nc.sbuf_tensor("wscr", [128, 1024], F32))
        wscri = EF(nc.sbuf_tensor("wscri", [128, 1024], I32))
        wdqC = [EF(nc.sbuf_tensor(f"wdqC{i}", [128, 1024], BF16)) for i in range(2)]

        WD_CH = ((0, 1024), (1024, 2048), (2048, IDIM))
        base_ws, base_df, base_wr = sWS.n, sDF[0].n, sWR.n
        wd_steps = []
        for hc_ in range(FCH):
            wd_steps += [("L", hc_, ci) for ci in range(3)]
            wd_steps += [("R", hc_, ci) for ci in range(3)]
            wd_steps += [("W", hc_, 0)]
            wd_steps += [("S", hc_, ci) for ci in range(3)]

        def wd_step(st):
            kind, hc, ci = st
            c0, c1 = WD_CH[ci]
            wdt = c1 - c0
            if kind == "L":
                w = []
                if hc == 0 and ci == 0:
                    w = [(sQE, n_qkdone), (sVE, n_ve)]
                elif hc > 0:
                    w = [(sWS, base_ws + 3 * (hc - 1) + ci + 1)]
                K.dma("q", w, wdl3[ci][:, 0:wdt],
                      wdd[hc * 128:(hc + 1) * 128, c0:c1], sDF[0])
            elif kind == "R":
                def wred(v, hc=hc, ci=ci, wdt=wdt):
                    v.reduce_max(sw[:, 25:26], wdl3[ci][:, 0:wdt],
                                 axis=AXX, apply_absolute_value=True)
                    v.drain()
                    if ci == 0:
                        return v.tensor_copy(wmaxd[:, hc:hc + 1], sw[:, 25:26])
                    return v.tensor_max(wmaxd[:, hc:hc + 1],
                                        wmaxd[:, hc:hc + 1], sw[:, 25:26])
                K.op("v", [(sDF[0], base_df + 16 * (3 * hc + ci + 1))], wred, sWR)
            elif kind == "W":
                def wrecip(v, hc=hc):
                    v.drain()
                    v.reciprocal(sw[:, 24:25], wmaxd[:, hc:hc + 1])
                    v.drain()
                    return v.tensor_scalar_mul(sw[:, 24:25], sw[:, 24:25], 127.0)
                w = [(sWS, base_ws + 3 * hc)] if hc > 0 else []
                K.op("v", w, wrecip, sWR)
            else:
                def wds_(ss, hc=hc, ci=ci, wdt=wdt):
                    ss.activation(wscr[:, 0:wdt], wdl3[ci][:, 0:wdt],
                                  ACT.Copy, scale=sw[:, 24:25])
                    ss.copy(wscri[:, 0:wdt], wscr[:, 0:wdt])
                    return ss.copy(wdqC[ci % 2][:, 0:wdt], wscri[:, 0:wdt])
                K.op("s", [(sWR, base_wr + 4 * (hc + 1)),
                           (sDQ, sDQ.n - 16)], wds_, sWS)
                K.dma("q", [(sWS, sWS.n)], wdq_dram[hc, :, c0:c1],
                      wdqC[ci % 2][:, 0:wdt], sDQ)

        mask_cnt_after = {}

        def emit_st(hl, g, kc, first_flash):
            nn = sM.n

            def st(t, hl=hl, g=g, kc=kc, nn=nn):
                ks = kc * 128
                t.matmul(st_ps[nn % 2][:, 0:512], KF[hl][:, ks:ks + 128],
                         QF[hl][:, g * QG:g * QG + 512], start=True, stop=True)
                return t.matmul(st_ps[nn % 2][:, 512:QG], KF[hl][:, ks:ks + 128],
                                QF[hl][:, g * QG + 512:(g + 1) * QG],
                                start=True, stop=True)
            w = [(sEx, sEx.n - 1)]
            if first_flash is not None:
                w += [(sR, rope_unit[hl])]
                if first_flash:
                    w.append((sVE, n_ve))
                    w.append((sQE, n_qkdone))
            K.op("t", w, st, sM)

        def emit_exp():
            nn = sEx.n
            K.op("s", [(sM, sM.n), (sPv, sPv.n)],
                 lambda s, nn=nn: s.activation(pt[nn % 3][:], st_ps[nn % 2][:],
                                               ACT.Exp), sEx)

        def emit_mask(g, kc, nglob):
            off = QG - (kc * 128 - g * QG)
            w = [(sEx, sEx.n)] + ([(sC, n_const)] if nglob == 0 else [])
            K.op("v", w,
                 lambda v, nn=nglob, off=off: v.tensor_mul(
                     pt[nn % 3][:], pt[nn % 3][:], band[:, off:off + QG]), sMk)
            mask_cnt_after[nglob] = sMk.n

        def emit_rspv(hl, g, kc, diag, first, last, nglob, gidx):
            if diag:
                w = [(sMk, mask_cnt_after[nglob])]
            else:
                w = [(sEx, nglob + 1)]
            if first and gidx > 0:
                w += [(sNm, sNm.n), (sRc, sRc.n)]

            def rp(t, hl=hl, kc=kc, nn=nglob, first=first, last=last):
                pp = pt[nn % 3]
                ks = kc * 128
                t.matmul(rsA[0:1, 0:512], onesB[:], pp[:, 0:512],
                         start=first, stop=last)
                t.matmul(rsB[0:1, 0:512], onesB[:], pp[:, 512:QG],
                         start=first, stop=last)
                t.matmul(o_ps[:, 0:512], VT[hl][:, ks:ks + 128],
                         pp[:, 0:512], start=first, stop=last)
                return t.matmul(o_ps[:, 512:QG], VT[hl][:, ks:ks + 128],
                                pp[:, 512:QG], start=first, stop=last)
            K.op("t", w, rp, sPv)

        def emit_tail(hl, g, gidx):
            def v1(v):
                v.tensor_copy(rw[0:1, 0:512], rsA[0:1, 0:512])
                return v.tensor_copy(rw[0:1, 512:QG], rsB[0:1, 0:512])
            K.op("v", [(sPv, sPv.n)], v1, sRc)

            def bc(t):
                t.matmul(rsA[:, 0:512], onesF[:], rw[0:1, 0:512],
                         start=True, stop=True)
                return t.matmul(rsB[:, 0:512], onesF[:], rw[0:1, 512:QG],
                                start=True, stop=True)
            K.op("t", [(sRc, sRc.n)], bc, sRc)

            def v2(v):
                with nc.allow_low_precision(reason="bf16 recip == old f32 recip + bf16 copy"):
                    v.reciprocal(rbc[:, 0:512], rsA[:])
                    return v.reciprocal(rbc[:, 512:QG], rsB[:])
            K.op("v", [(sRc, sRc.n)], v2, sRc)
            K.op("v", [(sAp[gidx % 2], 16 * max(0, (gidx - 2) // 2 + 1))],
                 lambda v, gidx=gidx: v.tensor_mul(obuf[gidx % 2][:], o_ps[:],
                                                   rbc[:]), sNm)
            K.dma("g", [(sNm, sNm.n)], a2a_in[hl][g], obuf[gidx % 2][:],
                  sAp[gidx % 2])

        gidx = 0
        nglob = 0
        for hl in range(2):
            for g in range(NG):
                nkc = 8 * (g + 1)
                base = nglob
                for i in range(nkc):
                    kc = i
                    diag = kc >= 8 * g
                    emit_st(hl, g, kc,
                            first_flash=(True if nglob == 0 else
                                         (False if (g == 0 and kc == 0) else None)))
                    emit_exp()
                    if diag:
                        emit_mask(g, kc, nglob)
                    if i >= 2:
                        j = i - 2
                        emit_rspv(hl, g, j, j >= 8 * g, j == 0, j == nkc - 1,
                                  base + j, gidx)
                    nglob += 1
                    if nglob >= 8 and nglob % 6 == 2 and wd_steps:
                        wd_step(wd_steps.pop(0))
                for j in (nkc - 2, nkc - 1):
                    emit_rspv(hl, g, j, j >= 8 * g, j == 0, j == nkc - 1,
                              base + j, gidx)
                emit_tail(hl, g, gidx)
                gidx += 1
            if hl == 0:
                A2A0_WAIT = (sAp[0].n, sAp[1].n)

        if "o0" in dbg:
            for gg in range(NG):
                K.dma("g", [(sAp[0], sAp[0].n), (sAp[1], sAp[1].n)],
                      dbg["o0"][:, gg * QG:(gg + 1) * QG], a2a_in[0][gg], sY)
        if "o1" in dbg:
            for gg in range(NG):
                K.dma("g", [(sAp[0], sAp[0].n), (sAp[1], sAp[1].n)],
                      dbg["o1"][:, gg * QG:(gg + 1) * QG], a2a_in[1][gg], sY)

        def cc0(gq):
            with allow_ici_all_to_all():
                return gq.collective_compute(
                    "AllToAll", AluOpType.bypass,
                    replica_groups=[list(range(NC))],
                    ins=[a2a_in[0]], outs=[a2a_out[0]])

        def cc1(gq):
            with allow_ici_all_to_all():
                return gq.collective_compute(
                    "AllToAll", AluOpType.bypass,
                    replica_groups=[list(range(NC))],
                    ins=[a2a_in[1]], outs=[a2a_out[1]])
        for st in wd_steps:
            wd_step(st)
        wd_steps = []

        K.op("g", [(sAp[0], A2A0_WAIT[0]), (sAp[1], A2A0_WAIT[1])], cc0, sCc)
        K.op("g", [(sAp[0], sAp[0].n), (sAp[1], sAp[1].n)], cc1, sCc)

        fl.close()
        att.close()

        h_sb = E(nc.sbuf_tensor("h_sb", [128, ICH * 2048], BF16))

        post = es.enter_context(contextlib.ExitStack())
        EP = post.enter_context
        zt = [EP(nc.sbuf_tensor(f"zt{i}", [128, M], F32)) for i in range(2)]

        scr = EP(nc.sbuf_tensor("scr", [128, M], F32))
        scri = EP(nc.sbuf_tensor("scri", [128, M], I32))
        xqt = EP(nc.sbuf_tensor("xqt", [128, M], BF16))
        misc_ps = EP(nc.psum_tensor("misc_ps", [128, 512], F32))

        postx = es.enter_context(contextlib.ExitStack())

        wgf = [postx.enter_context(nc.sbuf_tensor(f"wgf{i}", [128, M], F32))
               for i in range(2)]
        wuf = [postx.enter_context(nc.sbuf_tensor(f"wuf{i}", [128, M], F32))
               for i in range(2)]
        wgq = postx.enter_context(nc.sbuf_tensor("wgq", [128, M], BF16))
        wuq = postx.enter_context(nc.sbuf_tensor("wuq", [128, M], BF16))
        wgT = [postx.enter_context(nc.sbuf_tensor(f"wgT{i}", [128, M], BF16)) for i in range(3)]
        wuT = [postx.enter_context(nc.sbuf_tensor(f"wuT{i}", [128, M], BF16)) for i in range(3)]
        silu_sb = postx.enter_context(nc.sbuf_tensor("silu_sb", [128, 512], BF16))
        ups_sb = postx.enter_context(nc.sbuf_tensor("ups_sb", [128, 512], BF16))
        sgc_all = postx.enter_context(nc.sbuf_tensor("sgc_all", [128, ICH], F32))
        suc_all = postx.enter_context(nc.sbuf_tensor("suc_all", [128, ICH], F32))
        wmax_all = postx.enter_context(nc.sbuf_tensor("wmax_all", [128, 2 * ICH], F32))
        tw_ps = [postx.enter_context(nc.psum_tensor(f"tw_ps{i}", [128, 1024], BF16)) for i in range(2)]

        def prepare_load(ic):
            K.dma("q", [(sWS, sWS.n)], wgf[ic % 2][:],
                  wgd[ic * 128:(ic + 1) * 128, :], sLG)
            K.dma("q", [], wuf[ic % 2][:], wud[ic * 128:(ic + 1) * 128, :], sLU)

        WQS_N = {}

        def prepare_vs(ic):
            thunks = []
            for gi, (wf_, wqq) in enumerate(((wgf[ic % 2], wgq),
                                             (wuf[ic % 2], wuq))):
                tcol = 20 + (2 * (ic % 2) + gi)
                wcol = gi * ICH + ic

                def wqv(v, ic=ic, gi=gi, wf_=wf_, tcol=tcol, wcol=wcol):
                    v.reduce_max(wmax_all[:, wcol:wcol + 1], wf_[:], axis=AXX,
                                 apply_absolute_value=True)
                    v.drain()
                    v.reciprocal(sw[:, tcol:tcol + 1],
                                 wmax_all[:, wcol:wcol + 1])
                    v.drain()
                    return v.tensor_scalar_mul(sw[:, tcol:tcol + 1],
                                               sw[:, tcol:tcol + 1], 127.0)
                wv = [(sLG if gi == 0 else sLU, 16 * (ic + 1))]
                if (ic - 2, gi) in WQS_N:
                    wv.append((sWS, WQS_N[(ic - 2, gi)]))
                K.op("v", wv, wqv, sWq)
                n_wqv = sWq.n

                def em_act(ic=ic, gi=gi, wf_=wf_, tcol=tcol, n_wqv=n_wqv):
                    K.op("s", [(sWq, n_wqv), (sWt, sWt.n)],
                         lambda ss, wf_=wf_, tcol=tcol: ss.activation(
                             scr[:], wf_[:], ACT.Copy,
                             scale=sw[:, tcol:tcol + 1]), sWS)
                    WQS_N[(ic, gi)] = sWS.n

                def em_cpi():
                    K.op("s", [], lambda ss: ss.copy(scri[:], scr[:]), sWS)

                def em_cpb(wqq=wqq):
                    K.op("s", [], lambda ss, wqq=wqq: ss.copy(wqq[:], scri[:]),
                         sWS)
                thunks += [em_act, em_cpi, em_cpb]
            return thunks

        def prepare_tr(ic):
            for gi, (wqq, wT) in enumerate(((wgq, wgT), (wuq, wuT))):
                def wtr(t, wqq=wqq, gi=gi, ic=ic):
                    for fc in range(FCH):
                        mm = t.transpose(tw_ps[gi][:, fc * 128:(fc + 1) * 128],
                                         wqq[:, fc * 128:(fc + 1) * 128], identB[:])
                    return mm
                K.op("t", [(sWS, sWS.n), (sWe, sWe.n - 1)], wtr, sWt)
                K.op("v", [(sWt, sWt.n), (sG, sG.n - 4)],
                     lambda v, wT=wT, gi=gi, ic=ic: v.tensor_copy(wT[ic % 3][:],
                                                                  tw_ps[gi][:]), sWe)

        def prepare(ic):
            for th in prepare_vs(ic):
                th()
            prepare_tr(ic)

        K.op("v", [], lambda v: v.memset(sw[:, HMAX:HMAX + 1], 0.0), sH)
        K.dma("q", [(sAp[0], sAp[0].n), (sAp[1], sAp[1].n)], wgf[0][:],
              wgd[0:128, :], sLG)
        K.dma("q", [], wuf[0][:], wud[0:128, :], sLU)
        prepare_load(1)
        prepare(0)
        prepare(1)
        prepare_load(2)
        prepare_load(3)

        wop = postx.enter_context(contextlib.ExitStack())
        o_all = wop.enter_context(nc.sbuf_tensor("o_all", [128, 16 * QG], BF16))
        xr = [wop.enter_context(nc.sbuf_tensor(f"xr{i}", [128, M], F32)) for i in range(2)]
        sq_scr = wop.enter_context(nc.sbuf_tensor("sq_scr", [128, M], F32))
        wo_sb = wop.enter_context(nc.sbuf_tensor("wo_sb", [128, H * M], BF16))
        for h in range(H):
            K.dma("g", [], wo_sb[:, h * M:(h + 1) * M], wo4[h], sWO)
        n_wo = sWO.n
        attn_ps = [wop.enter_context(nc.psum_tensor(f"attn_ps{i}", [128, M], F32))
                   for i in range(2)]

        for hl in range(2):
            for j in range(NC):
                col = ((j // 4) * 8 + 2 * (j % 4) + hl) * QG
                K.dma("g", [(sCc, sCc.n - (1 if hl == 0 else 0))],
                      o_all[:, col:col + QG], a2a_out[hl][j], sOa)
        n_oall = sOa.n

        if "oall" in dbg:
            K.dma("g", [(sOa, n_oall)], dbg["oall"], o_all[:], sY)

        K.op("v", [], lambda v: v.memset(sw[:, ZMAX:ZMAX + 1], 0.0), sZv)

        for ti in range(NT):
            b_, tc = ti // 8, ti % 8
            K.dma("q", [(sZv, sZv.n - 3)], xr[ti % 2][:],
                  xres[b_, tc * 128:(tc + 1) * 128, :], sXR[ti % 2])

            def womm(t, b_=b_, tc=tc, ti=ti):
                for h in range(H):
                    col = (b_ * 8 + h) * QG + tc * 128
                    t.matmul(attn_ps[ti % 2][:, 0:512], o_all[:, col:col + 128],
                             wo_sb[:, h * M:h * M + 512],
                             start=(h == 0), stop=(h == H - 1))
                    mm = t.matmul(attn_ps[ti % 2][:, 512:M], o_all[:, col:col + 128],
                                  wo_sb[:, h * M + 512:(h + 1) * M],
                                  start=(h == 0), stop=(h == H - 1))
                return mm
            K.op("t", [(sOa, n_oall), (sWO, n_wo), (sZv, sZv.n - 3)], womm, sWo)

            def va(v, ti=ti):
                zb = zt[ti % 2]
                v.scalar_tensor_tensor(zb[:, 0:512], attn_ps[ti % 2][:, 0:512],
                                       1.0, xr[ti % 2][:, 0:512], MULT, ADD)
                v.scalar_tensor_tensor(zb[:, 512:M], attn_ps[ti % 2][:, 512:M],
                                       1.0, xr[ti % 2][:, 512:M], MULT, ADD)
                v.tensor_mul(sq_scr[:], zb[:], zb[:])
                v.reduce_sum(sw[:, MEAN:MEAN + 1], zb[:], axis=AXX)
                v.reduce_sum(sw[:, SSQ:SSQ + 1], sq_scr[:], axis=AXX)
                v.drain()
                mp = MEAN0 + ti % 2
                v.tensor_scalar_mul(sw[:, mp:mp + 1], sw[:, MEAN:MEAN + 1], 1.0 / M)
                v.drain()
                v.tensor_mul(sw[:, T1:T1 + 1], sw[:, mp:mp + 1], sw[:, mp:mp + 1])
                v.drain()
                return v.scalar_tensor_tensor(sw[:, VAR:VAR + 1],
                                              sw[:, SSQ:SSQ + 1], 1.0 / M,
                                              sw[:, T1:T1 + 1], MULT, SUB)
            K.op("v", [(sWo, sWo.n), (sXR[ti % 2], 16 * (ti // 2 + 1)),
                       (sSt, sSt.n),
                       (sZS[ti % 2], 16 * max(0, (ti - 2) // 2 + 1))], va, sZv)
            if "zpre" in dbg:
                K.dma("q", [(sZv, sZv.n)], dbg["zpre"][ti], zt[ti % 2][:], sY)
            vb_extra = [(sY, sY.n)] if "zpre" in dbg else []
            K.op("s", [(sZv, sZv.n)],
                 lambda s, ti=ti: s.activation(sw[:, STD0 + ti % 2:STD0 + ti % 2 + 1],
                                               sw[:, VAR:VAR + 1], ACT.Sqrt,
                                               bias=eps_col[:]), sSt)

            def vb(v, ti=ti):
                zb = zt[ti % 2]
                v.reciprocal(sw[:, INV:INV + 1], sw[:, STD0 + ti % 2:STD0 + ti % 2 + 1])
                v.drain()
                v.tensor_scalar(zb[:], zb[:], sw[:, MEAN0 + ti % 2:MEAN0 + ti % 2 + 1],
                                sw[:, INV:INV + 1], SUB, MULT)
                v.reduce_max(sw[:, T1:T1 + 1], zb[:], axis=AXX,
                             apply_absolute_value=True)
                v.drain()
                return v.tensor_max(sw[:, ZMAX:ZMAX + 1], sw[:, ZMAX:ZMAX + 1],
                                    sw[:, T1:T1 + 1])
            K.op("v", [(sSt, sSt.n)] + vb_extra, vb, sZv)
            K.dma("q", [(sZv, sZv.n)], z_dram[ti], zt[ti % 2][:], sZS[ti % 2])
            if "z" in dbg:
                K.dma("q", [], dbg["z"][ti], zt[ti % 2][:], sY)

        if "swd" in dbg:
            K.dma("q", [(sZv, sZv.n)], dbg["swd"], sw[:], sY)
        K.op("t", [(sZv, sZv.n)],
             lambda t: t.transpose(misc_ps[0:1, 0:128], sw[:, ZMAX:ZMAX + 1],
                                   identF[:]), sWt)
        if "mps" in dbg:
            def mpscp(v):
                return v.tensor_copy(scr[:, 0:512], misc_ps[:])
            K.op("v", [(sWt, sWt.n)], mpscp, sGl)
            K.dma("q", [(sGl, sGl.n)], dbg["mps"], scr[:, 0:512], sY)
        K.op("v", [(sWt, sWt.n)],
             lambda v: v.reduce_max(rw2[0:1, 0:1], misc_ps[0:1, 0:128], axis=AXX), sGl)
        K.dma("g", [(sGl, sGl.n)], cc_in[0:1, 0:1], rw2[0:1, 0:1], sGB)

        K.op("g", [(sGB, sGB.n)],
             lambda g: g.collective_compute("AllReduce", AMAX,
                                            replica_groups=[list(range(NC))],
                                            ins=[cc_in], outs=[cc_out]), sCc)
        K.dma("g", [(sCc, sCc.n)], rw2[0:1, 4:5], cc_out[0:1, 0:1], sGB)
        n_gm1 = sGB.n
        K.op("s", [(sC, n_const)],
             lambda s: s.activation(rw2[0:1, 5:6], clip_sb[0:1, 0:1], ACT.Sigmoid),
             sSt)

        def rx_chain(v):
            v.tensor_mul(rw2[0:1, 6:7], rw2[0:1, 4:5], rw2[0:1, 5:6])
            v.drain()
            v.reciprocal(rw2[0:1, 7:8], rw2[0:1, 6:7])
            v.drain()
            v.tensor_scalar_mul(rw2[0:1, 7:8], rw2[0:1, 7:8], 127.0)
            v.drain()
            return v.tensor_scalar_mul(rw2[0:1, 8:9], rw2[0:1, 6:7],
                                       1.0 / (127.0 * 127.0))
        K.op("v", [(sGB, n_gm1), (sSt, sSt.n)], rx_chain, sGl)

        def bc1(t):
            t.matmul(misc_ps[:, 0:1], onesF[:], rw2[0:1, 7:8], start=True, stop=True)
            return t.matmul(misc_ps[:, 1:2], onesF[:], rw2[0:1, 8:9],
                            start=True, stop=True)
        K.op("t", [(sGl, sGl.n)], bc1, sWt)

        def bc1c(v):
            v.tensor_copy(sw[:, RXC:RXC + 1], misc_ps[:, 0:1])
            return v.tensor_copy(sw[:, SXD:SXD + 1], misc_ps[:, 1:2])
        K.op("v", [(sWt, sWt.n)], bc1c, sGl)
        n_gl1 = sGl.n
        if "gmax" in dbg:
            K.dma("q", [(sGl, n_gl1)], dbg["gmax"][0:1, 0:4], rw2[0:1, 4:8], sY)

        wop.close()

        xq_fm = postx.enter_context(nc.sbuf_tensor("xq_fm", [128, FCH * 2048], BF16))
        qscope = postx.enter_context(contextlib.ExitStack())
        tq_ps = [qscope.enter_context(nc.psum_tensor(f"tq_ps{i}", [128, 1024], BF16))
                 for i in range(2)]
        for ti in range(NT):
            w = [(sXq, sXq.n - 1)]
            if ti < 2:
                w += [(sZS[0], sZS[0].n), (sZS[1], sZS[1].n)]
            K.dma("q", w, zt[ti % 2][:], z_dram[ti], sZR[ti % 2])

            def qv(v, ti=ti):
                v.tensor_scalar_mul(scr[:], zt[ti % 2][:], sw[:, RXC:RXC + 1])
                v.tensor_copy(scri[:], scr[:])
                v.tensor_scalar(scri[:], scri[:], -128, 127, AMAX, AMIN)
                return v.tensor_copy(xqt[:], scri[:])
            K.op("v", [(sZR[ti % 2], 16 * (ti // 2 + 1)), (sGl, n_gl1), (sTq, sTq.n - 1)], qv, sXq)
            for fc in range(FCH):
                K.op("t", [(sXq, sXq.n), (sDe, sDe.n - 1)],
                     lambda t, ti=ti, fc=fc: t.transpose(
                         tq_ps[fc % 2][:, 0:128], xqt[:, fc * 128:(fc + 1) * 128],
                         identB[:]), sTq)
                K.op("v", [(sTq, sTq.n)],
                     lambda v, ti=ti, fc=fc: v.tensor_copy(
                         xq_fm[:, fc * 2048 + ti * 128:fc * 2048 + (ti + 1) * 128],
                         tq_ps[fc % 2][:, 0:128]), sDe)
        n_xq = sDe.n
        if "xq" in dbg:
            K.dma("g", [(sDe, n_xq)], dbg["xq"], xq_fm[:], sY)
        qscope.close()

        ffa = postx.enter_context(contextlib.ExitStack())
        EG = ffa.enter_context
        gu_ps = [EG(nc.psum_tensor(f"gu_ps{i}", [128, 1024], F32)) for i in range(2)]

        def compute(ic, s_extras=()):
            def sgcmul(v, ic=ic):
                v.tensor_mul(sgc_all[:, ic:ic + 1],
                             wmax_all[:, ic:ic + 1], sw[:, SXD:SXD + 1])
                return v.tensor_mul(suc_all[:, ic:ic + 1],
                                    wmax_all[:, ICH + ic:ICH + ic + 1],
                                    sw[:, SXD:SXD + 1])
            K.op("v", [(sGl, n_gl1), (sWq, 2 * (ic + 1))], sgcmul, sSg)
            for tg4 in range(4):
                def gu(t, ic=ic, tg4=tg4):
                    for fc in range(FCH):
                        t.matmul(gu_ps[tg4 % 2][:, 0:512],
                                 wgT[ic % 3][:, fc * 128:(fc + 1) * 128],
                                 xq_fm[:, fc * 2048 + tg4 * 512:fc * 2048 + (tg4 + 1) * 512],
                                 start=(fc == 0), stop=(fc == FCH - 1))
                    for fc in range(FCH):
                        mm = t.matmul(gu_ps[tg4 % 2][:, 512:1024],
                                      wuT[ic % 3][:, fc * 128:(fc + 1) * 128],
                                      xq_fm[:, fc * 2048 + tg4 * 512:fc * 2048 + (tg4 + 1) * 512],
                                      start=(fc == 0), stop=(fc == FCH - 1))
                    return mm
                K.op("t", [(sWe, sWe.n - 2 * (min(ic + 2, ICH) - 1 - ic)),
                           (sDe, n_xq), (sH, sH.n - 1)], gu, sG)
                K.op("s", [(sG, sG.n), (sH, sH.n), (sSg, sSg.n)],
                     lambda s, ic=ic, tg4=tg4: s.activation(
                         silu_sb[:], gu_ps[tg4 % 2][:, 0:512], ACT.Silu,
                         scale=sgc_all[:, ic:ic + 1]), sSi)

                K.op("s", [(sH, sH.n)],
                     lambda s, ic=ic, tg4=tg4: s.activation(
                         ups_sb[:], gu_ps[tg4 % 2][:, 512:1024], ACT.Copy,
                         scale=suc_all[:, ic:ic + 1]), sSi)

                for th in s_extras[2 * tg4:2 * tg4 + 2]:
                    th()

                def hv(v, ic=ic, tg4=tg4):
                    col = ic * 2048 + tg4 * 512
                    mm = v.tensor_mul(h_sb[:, col:col + 512], silu_sb[:], ups_sb[:])
                    if tg4 == 3:
                        v.reduce_max(sw[:, T1:T1 + 1],
                                     h_sb[:, ic * 2048:(ic + 1) * 2048], axis=AXX,
                                     apply_absolute_value=True)
                        v.drain()
                        mm = v.tensor_max(sw[:, HMAX:HMAX + 1],
                                          sw[:, HMAX:HMAX + 1], sw[:, T1:T1 + 1])
                    return mm
                K.op("v", [(sSi, sSi.n)], hv, sH)

        for ic in range(ICH):
            ext = prepare_vs(ic + 2) if ic + 2 < ICH else ()
            compute(ic, ext)
            if ic + 2 < ICH:
                prepare_tr(ic + 2)
            if ic + 4 < ICH:
                prepare_load(ic + 4)

        if "sgc" in dbg:
            def sgdump(q):
                return q.dma_start(out=dbg["sgc"][:, 0:2 * ICH], in_=wmax_all[:])
            K.op("q", [(sH, sH.n)], sgdump, sY, 16)
            K.dma("q", [], dbg["sgc"][:, 2 * ICH:3 * ICH], sgc_all[:], sY)
            K.dma("q", [], dbg["sgc"][:, 3 * ICH:4 * ICH], suc_all[:], sY)

        K.op("t", [(sH, sH.n)],
             lambda t: t.transpose(misc_ps[0:1, 0:128], sw[:, HMAX:HMAX + 1],
                                   identF[:]), sWt)
        K.op("v", [(sWt, sWt.n)],
             lambda v: v.reduce_max(rw2[0:1, 9:10], misc_ps[0:1, 0:128], axis=AXX),
             sGl)
        K.dma("g", [(sGl, sGl.n)], cc2_in[0:1, 0:1], rw2[0:1, 9:10], sGB)
        K.op("g", [(sGB, sGB.n)],
             lambda g: g.collective_compute("AllReduce", AMAX,
                                            replica_groups=[list(range(NC))],
                                            ins=[cc2_in], outs=[cc2_out]), sCc)
        K.dma("g", [(sCc, sCc.n)], rw2[0:1, 10:11], cc2_out[0:1, 0:1], sGB)
        n_gm2 = sGB.n
        K.op("s", [(sC, n_const)],
             lambda s: s.activation(rw2[0:1, 11:12], clip_sb[0:1, 1:2], ACT.Sigmoid),
             sSt)

        def rh_chain(v):
            v.tensor_mul(rw2[0:1, 12:13], rw2[0:1, 10:11], rw2[0:1, 11:12])
            v.drain()
            v.reciprocal(rw2[0:1, 13:14], rw2[0:1, 12:13])
            v.drain()
            v.tensor_scalar_mul(rw2[0:1, 13:14], rw2[0:1, 13:14], 127.0)
            v.drain()
            return v.tensor_scalar_mul(rw2[0:1, 14:15], rw2[0:1, 12:13],
                                       1.0 / (127.0 * 127.0))
        K.op("v", [(sGB, n_gm2), (sSt, sSt.n)], rh_chain, sGl)

        def bc2(t):
            t.matmul(misc_ps[:, 2:3], onesF[:], rw2[0:1, 13:14], start=True, stop=True)
            return t.matmul(misc_ps[:, 3:4], onesF[:], rw2[0:1, 14:15],
                            start=True, stop=True)
        K.op("t", [(sGl, sGl.n)], bc2, sWt)

        def bc2c(v):
            v.tensor_copy(sw[:, RHC:RHC + 1], misc_ps[:, 2:3])
            return v.tensor_copy(sw[:, SHD:SHD + 1], misc_ps[:, 3:4])
        K.op("v", [(sWt, sWt.n)], bc2c, sGl)
        n_gl2 = sGl.n
        if "gmax" in dbg:
            K.dma("q", [(sGl, n_gl2)], dbg["gmax"][0:1, 4:8], rw2[0:1, 10:14], sY)


        ffa.close()
        postx.close()

        ffd = post.enter_context(contextlib.ExitStack())
        ED = ffd.enter_context
        wdT = ED(nc.sbuf_tensor("wdT", [128, ICH * M], BF16))
        wdqD = [ED(nc.sbuf_tensor(f"wdqD{i}", [128, IDIM], BF16)) for i in range(2)]
        row_sb = ED(nc.sbuf_tensor("row_sb", [1, M], F32))
        bcast_sb = ED(nc.sbuf_tensor("bcast_sb", [128, M], BF16))
        va2scr = ED(nc.sbuf_tensor("va2scr", [128, M], F32))
        twd_ps = [ED(nc.psum_tensor(f"twd_ps{i}", [128, 128], BF16)) for i in range(2)]
        dn_ps = [ED(nc.psum_tensor(f"dn_ps{i}", [128, M], F32)) for i in range(2)]
        wdTi3 = wdT[:].rearrange("p (i m) -> p i m", i=ICH)
        n_dq = sDQ.n

        for hc in range(FCH):
            w = [(sDQ, n_dq), (sWt, sWt.n - 22)]
            if hc == 0:
                w.append((sH, sH.n))
            K.dma("q", w, wdqD[hc % 2][:], wdq_dram[hc], sDW[hc % 2])
            for ic in range(ICH):
                K.op("t", [(sDW[hc % 2], 16 * (hc // 2 + 1)), (sWe, sWe.n - 1)],
                     lambda t, hc=hc, ic=ic: t.transpose(
                         twd_ps[ic % 2][:, 0:128],
                         wdqD[hc % 2][:, ic * 128:(ic + 1) * 128], identB[:]), sWt)
                K.op("v", [(sWt, sWt.n)],
                     lambda v, hc=hc, ic=ic: v.tensor_copy(
                         wdT[:, ic * M + hc * 128:ic * M + (hc + 1) * 128],
                         twd_ps[ic % 2][:, 0:128]), sWe)

        def sdctr(t, h0):
            for hc in range(h0, h0 + 4):
                mm = t.transpose(misc_ps[0:1, (hc - h0) * 128:(hc - h0 + 1) * 128],
                                 wmaxd[:, hc:hc + 1], identF[:])
            return mm
        K.op("t", [(sWR, sWR.n), (sGl, n_gl2)],
             lambda t: sdctr(t, 0), sWt)
        K.op("v", [(sWt, sWt.n)],
             lambda v: v.tensor_copy(row_sb[0:1, 0:512], misc_ps[0:1, 0:512]), sGl)
        K.op("t", [(sGl, sGl.n)], lambda t: sdctr(t, 4), sWt)

        def sdcrow(v):
            v.tensor_copy(row_sb[0:1, 512:M], misc_ps[0:1, 0:512])
            v.drain()
            return v.tensor_scalar_mul(row_sb[0:1, :], row_sb[0:1, :],
                                       sw[0:1, SHD:SHD + 1])
        K.op("v", [(sWt, sWt.n)], sdcrow, sGl)

        def sdcbc(t):
            t.matmul(dn_ps[0][:, 0:512], onesF[:], row_sb[0:1, 0:512],
                     start=True, stop=True)
            return t.matmul(dn_ps[0][:, 512:M], onesF[:], row_sb[0:1, 512:M],
                            start=True, stop=True)
        K.op("t", [(sGl, sGl.n)], sdcbc, sWt)
        K.op("v", [(sWt, sWt.n)],
             lambda v: v.tensor_copy(bcast_sb[:], dn_ps[0][:]), sGl)
        n_bcast = sGl.n

        for ic in range(ICH):
            K.op("v", [(sGl, n_bcast)] if ic == 0 else [],
                 lambda v, ic=ic: v.tensor_mul(wdTi3[:, ic, :], wdTi3[:, ic, :],
                                               bcast_sb[:]), sWe)
        n_wdT = sWe.n
        n_wdq = sWq.n
        if "row" in dbg:
            K.dma("q", [(sGl, n_bcast)], dbg["row"], row_sb[:], sY)
        if "wmaxd" in dbg:
            K.dma("q", [(sWq, n_wdq)], dbg["wmaxd"], wmaxd[:], sY)
        if "wdt" in dbg:
            K.dma("g", [(sWe, n_wdT)], dbg["wdt"], wdT[:], sY)
        h3 = h_sb[:].rearrange("p (i t) -> p i t", i=ICH)
        HB = ((0, 8), (8, 16), (16, ICH))

        def emit_hr(ti):
            for ci, (i0, i1) in enumerate(HB):
                nel = (i1 - i0) * 128

                def hs_ap(ti=ti, i0=i0, i1=i1):
                    return h3[:, i0:i1, ti * 128:(ti + 1) * 128]

                def hmul(ss, ti=ti, i0=i0, i1=i1, nel=nel):
                    return ss.mul(scr[:, 0:nel].rearrange(
                        "p (a b) -> p a b", a=i1 - i0), hs_ap(ti, i0, i1),
                        sw[:, RHC:RHC + 1])
                w = [(sGl, n_gl2), (sWq, n_wdq)]
                if sHr.n > 0:
                    w.append((sHr, sHr.n - 0))
                K.op("s", w, hmul, sHs)

                def hcvt(v, nel=nel):
                    v.tensor_copy(scri[:, 0:nel], scr[:, 0:nel])
                    return v.tensor_scalar(scri[:, 0:nel], scri[:, 0:nel],
                                           -128, 127, AMAX, AMIN)
                K.op("v", [(sHs, sHs.n)], hcvt, sHr)

                def hback(ss, ti=ti, i0=i0, i1=i1, nel=nel):
                    return ss.copy(hs_ap(ti, i0, i1), scri[:, 0:nel].rearrange(
                        "p (a b) -> p a b", a=i1 - i0))
                K.op("s", [(sHr, sHr.n)], hback, sHs)

        emit_hr(0)
        emit_hr(1)

        for ti in range(NT):
            b_, tc = ti // 8, ti % 8
            w = [(sXq, sXq.n), (sZv, sZv.n - 4),
                 (sYP[ti % 2], 16 * max(0, (ti - 2) // 2 + 1))]
            if ti < 2:
                w += [(sZR[0], sZR[0].n), (sZR[1], sZR[1].n)]
            K.dma("q", w, zt[ti % 2][:], z_dram[ti], sZD[ti % 2])

            def dmm(t, ti=ti):
                for ic in range(ICH):
                    hcol = ic * 2048 + ti * 128
                    t.matmul(dn_ps[ti % 2][:, 0:512], h_sb[:, hcol:hcol + 128],
                             wdT[:, ic * M:ic * M + 512],
                             start=(ic == 0), stop=(ic == ICH - 1))
                    mm = t.matmul(dn_ps[ti % 2][:, 512:M], h_sb[:, hcol:hcol + 128],
                                  wdT[:, ic * M + 512:(ic + 1) * M],
                                  start=(ic == 0), stop=(ic == ICH - 1))
                return mm
            K.op("t", [(sWe, n_wdT), (sHs, 6 * (ti + 1)), (sZv, sZv.n - 3)],
                 dmm, sD)
            if ti + 2 < NT:
                emit_hr(ti + 2)

            def va2(v, ti=ti):
                zb = zt[ti % 2]
                v.scalar_tensor_tensor(zb[:, 0:512], dn_ps[ti % 2][:, 0:512],
                                       1.0, zb[:, 0:512], MULT, ADD)
                v.scalar_tensor_tensor(zb[:, 512:M], dn_ps[ti % 2][:, 512:M],
                                       1.0, zb[:, 512:M], MULT, ADD)
                v.tensor_mul(va2scr[:], zb[:], zb[:])
                v.reduce_sum(sw[:, MEAN:MEAN + 1], zb[:], axis=AXX)
                v.reduce_sum(sw[:, SSQ:SSQ + 1], va2scr[:], axis=AXX)
                v.drain()
                mp = MEAN0 + ti % 2
                v.tensor_scalar_mul(sw[:, mp:mp + 1], sw[:, MEAN:MEAN + 1], 1.0 / M)
                v.drain()
                v.tensor_mul(sw[:, T1:T1 + 1], sw[:, mp:mp + 1], sw[:, mp:mp + 1])
                v.drain()
                return v.scalar_tensor_tensor(sw[:, VAR:VAR + 1],
                                              sw[:, SSQ:SSQ + 1], 1.0 / M,
                                              sw[:, T1:T1 + 1], MULT, SUB)
            K.op("v", [(sD, sD.n), (sZD[ti % 2], 16 * (ti // 2 + 1)), (sSt, sSt.n), (sWq, n_wdq)],
                 va2, sZv)
            K.op("s", [(sZv, sZv.n)],
                 lambda s, ti=ti: s.activation(sw[:, STD0 + ti % 2:STD0 + ti % 2 + 1],
                                               sw[:, VAR:VAR + 1], ACT.Sqrt,
                                               bias=eps_col[:]), sSt)

            def vb2(v, ti=ti):
                zb = zt[ti % 2]
                v.reciprocal(sw[:, INV:INV + 1], sw[:, STD0 + ti % 2:STD0 + ti % 2 + 1])
                v.drain()
                return v.tensor_scalar(zb[:], zb[:],
                                       sw[:, MEAN0 + ti % 2:MEAN0 + ti % 2 + 1],
                                       sw[:, INV:INV + 1], SUB, MULT)
            K.op("v", [(sSt, sSt.n)], vb2, sZv)
            K.dma("q", [(sZv, sZv.n)], y_out[b_, tc * 128:(tc + 1) * 128, :],
                  zt[ti % 2][:], sYP[ti % 2])

        if "hh" in dbg:
            K.dma("g", [(sHs, sHs.n)], dbg["hh"], h_sb[:], sY)
        n_y0, n_y1, n_yd = sYP[0].n, sYP[1].n, sY.n

        def _fence(eng):
            eng.wait_ge(sYP[0].h, n_y0)
            eng.wait_ge(sYP[1].h, n_y1)
            if n_yd:
                eng.wait_ge(sY.h, n_yd)
        K.ops["q"].append(_fence)

        @block.tensor
        def _(t):
            for fn in K.ops["t"]:
                fn(t)

        @block.scalar
        def _(s):
            for fn in K.ops["s"]:
                fn(s)

        @block.vector
        def _(v):
            for fn in K.ops["v"]:
                fn(v)

        @block.sync
        def _(q):
            for fn in K.ops["q"]:
                fn(q)

        @block.gpsimd
        def _(g):
            for fn in K.ops["g"]:
                fn(g)

    return nc


_NC_CACHE = {}


def _get_nc(debug=()):
    key = tuple(sorted(debug))
    if key not in _NC_CACHE:
        _NC_CACHE[key] = build(debug=key)
    return _NC_CACHE[key]


def kernel(L0_attn_x, L0_attn_wq, L0_attn_wk, L0_attn_wv, L0_attn_wo,
           L0_ffn_clip_x, L0_ffn_clip_down, L0_ffn_w_gate, L0_ffn_w_up,
           L0_ffn_w_down, _debug=()):
    x = np.ascontiguousarray(np.asarray(L0_attn_x, dtype=np.float32))
    wq = np.ascontiguousarray(np.asarray(L0_attn_wq, dtype=np.float32))
    wk = np.ascontiguousarray(np.asarray(L0_attn_wk, dtype=np.float32))
    wv = np.ascontiguousarray(np.asarray(L0_attn_wv, dtype=np.float32))
    wo = np.ascontiguousarray(np.asarray(L0_attn_wo, dtype=np.float32))
    wg = np.ascontiguousarray(np.asarray(L0_ffn_w_gate, dtype=np.float32))
    wu = np.ascontiguousarray(np.asarray(L0_ffn_w_up, dtype=np.float32))
    wd = np.ascontiguousarray(np.asarray(L0_ffn_w_down, dtype=np.float32))
    clip2 = np.stack(
        [np.asarray(L0_ffn_clip_x, dtype=np.float32).reshape(()),
         np.asarray(L0_ffn_clip_down, dtype=np.float32).reshape(())]
    ).reshape(1, 2)

    nc_prog = _get_nc(_debug)
    in_maps = []
    for c in range(NC):
        b = c // 4
        h0 = 2 * (c % 4)
        sl = slice(c * SL, (c + 1) * SL)
        in_maps.append({
            "xb": x[b],
            "xres": np.ascontiguousarray(x[:, sl, :]),
            "wq3": np.ascontiguousarray(wq[:, h0:h0 + 2, :]),
            "wk3": np.ascontiguousarray(wk[:, h0:h0 + 2, :]),
            "wv3": np.ascontiguousarray(wv[:, h0:h0 + 2, :]),
            "wo4": wo,
            "wgd": wg,
            "wud": wu,
            "wdd": wd,
            "clips": clip2,
        })
    res = run_bass_kernel_spmd(nc_prog, in_maps, core_ids=list(range(NC)))
    y = np.empty((B, T, M), dtype=np.float32)
    for c in range(NC):
        sl = slice(c * SL, (c + 1) * SL)
        y[:, sl, :] = res.results[c]["y"]
    if _debug:
        return y, res
    return y
